# version 15
# baseline (speedup 1.0000x reference)
import numpy as np
import ml_dtypes
from contextlib import ExitStack
import concourse.bass as bass
import concourse.mybir as mybir
from concourse.bass_utils import run_bass_kernel_spmd

F32 = mybir.dt.float32
BF16 = mybir.dt.bfloat16
AF = mybir.ActivationFunctionType
ALU = mybir.AluOpType
AX = mybir.AxisListType.X

B, T, D = 4, 4096, 1024
H, DH = 16, 64
NP = 8
OWN = 2048
NBLK, BLK = 8, 512
TT2 = 256
DFF = 4096
EPS_LN = 1e-5

ENGINES = ("sync", "vector", "scalar", "tensor", "gpsimd")


class Ctr:
    def __init__(self, nc, name):
        self.h = nc.alloc_semaphore(name)
        self.v = 0

    def bump(self, n=1):
        self.v += n
        return (self.h, n)


class Rec:

    def __init__(self):
        self.q = {e: [] for e in ENGINES}

    def op(self, eng, meth, *a, inc=None, **kw):
        self.q[eng].append(("op", meth, a, kw, inc))

    def wait(self, eng, ctr, val=None):
        v = ctr.v if val is None else val
        if v > 0:
            self.q[eng].append(("wait", ctr.h, v))

    def emit(self, nc):
        with nc.allow_low_precision(reason="bf16 matmul operands with fp32 accumulation (sanctioned)"), \
             nc.Block() as block:
            for ename in ENGINES:
                items = self.q[ename]
                if not items:
                    continue

                def mk(items=items):
                    def body(e):
                        for it in items:
                            if it[0] == "wait":
                                e.wait_ge(it[1], it[2])
                            else:
                                _, meth, a, kw, inc = it
                                rr = getattr(e, meth)(*a, **kw)
                                if inc is not None:
                                    rr.then_inc(inc[0], inc[1])
                    return body

                getattr(block, ename)(mk())


def build_program():
    nc = bass.Bass("TRN2", target_bir_lowering=False, detect_race_conditions=False)

    x_d = nc.dram_tensor("x", [T, D], F32, kind="ExternalInput").ap()
    wqkv_d = nc.dram_tensor("wqkv", [D, 3 * D], F32, kind="ExternalInput").ap()
    wout_d = nc.dram_tensor("wout", [D, D], F32, kind="ExternalInput").ap()
    bout_d = nc.dram_tensor("bout", [1, D], F32, kind="ExternalInput").ap()
    wfc_d = nc.dram_tensor("wfc", [DFF, D], F32, kind="ExternalInput").ap()
    lnw_d = nc.dram_tensor("lnw", [128, 32], F32, kind="ExternalInput").ap()
    lnb_d = nc.dram_tensor("lnb", [128, 32], F32, kind="ExternalInput").ap()
    wproj_d = nc.dram_tensor("wproj", [D, DFF], F32, kind="ExternalInput").ap()
    cos_hi_d = nc.dram_tensor("cos_hi", [128, T], F32, kind="ExternalInput").ap()
    cos_lo_d = nc.dram_tensor("cos_lo", [128, T], F32, kind="ExternalInput").ap()
    sin_hi_d = nc.dram_tensor("sin_hi", [128, T], F32, kind="ExternalInput").ap()
    sin_lo_d = nc.dram_tensor("sin_lo", [128, T], F32, kind="ExternalInput").ap()
    out_d = nc.dram_tensor("out", [OWN, D], F32, kind="ExternalOutput").ap()

    _scratch_kind = "ExternalOutput" if globals().get("_DBG_IO") else "Internal"
    xn_s = nc.dram_tensor("xn_s", [128, 8, OWN], BF16, kind=_scratch_kind).ap()
    k_s = nc.dram_tensor("k_s", [128, NP, T], BF16, kind=_scratch_kind).ap()
    q_s = nc.dram_tensor("q_s", [128, NP, OWN], BF16, kind=_scratch_kind).ap()
    v_s = nc.dram_tensor("v_s", [128, NP, 32 * 256], BF16, kind=_scratch_kind).ap()

    ident_np = np.eye(128, dtype=ml_dtypes.bfloat16)
    pm = np.zeros((128, 128), np.float32)
    for base in (0, 64):
        for m in range(32):
            pm[base + m + 32, base + m] = -1.0
            pm[base + m, base + m + 32] = 1.0
    ident_bf = nc.inline_tensor(ident_np, "c_ident").ap()
    pmat_bf = nc.inline_tensor(pm.astype(ml_dtypes.bfloat16), "c_pmat").ap()
    ones_bf = nc.inline_tensor(np.ones((128, 128), dtype=ml_dtypes.bfloat16), "c_ones").ap()
    ones_f32 = nc.inline_tensor(np.ones((128, 128), dtype=np.float32), "c_onesf").ap()
    c64_bf = nc.inline_tensor(np.full((128, 128), 1.0 / 64, dtype=ml_dtypes.bfloat16), "c_c64").ap()

    ges = ExitStack()

    def gsb(name, shape, dt):
        return ges.enter_context(nc.sbuf_tensor(name, shape, dt))

    g_ident = gsb("g_ident", [128, 128], BF16)
    g_pmat = gsb("g_pmat", [128, 128], BF16)
    g_ones_bf = gsb("g_ones_bf", [128, 128], BF16)
    g_ones_f = gsb("g_ones_f", [128, 128], F32)
    g_c64 = gsb("g_c64", [128, 128], BF16)
    g_lnw = gsb("g_lnw", [128, 32], F32)
    g_lnb = gsb("g_lnb", [128, 32], F32)
    g_brow = gsb("g_brow", [1, D], BF16)
    wc_buf = [gsb(f"wc_buf{i}", [128, 1024], F32) for i in range(4)]
    wqn_buf = [gsb(f"wqn_buf{i}", [128, 1024], BF16) for i in range(2)]
    red96 = gsb("red96", [128, 384], F32)
    acc1 = gsb("acc1", [128, 1], F32)
    thr_fc = gsb("thr_fc", [128, 2], F32)
    thr_pj = gsb("thr_pj", [128, 2], F32)
    thr11 = gsb("thr11", [1, 1], F32)

    C = lambda n: Ctr(nc, n)
    s_const = C("s_const")
    s_cg = C("s_cg")
    s_x = C("s_x")
    s_tab = C("s_tab")
    s_wq = C("s_wq")
    s_w2 = [C("s_w2a"), C("s_w2b"), C("s_w2c"), C("s_w2d")]
    s_xn2 = [C("s_xn2a"), C("s_xn2b")]
    s_kvq = [C("s_kvqa"), C("s_kvqb")]
    s_res = [C("s_resa"), C("s_resb")]
    s_w4 = C("s_w4")
    s_fin = C("s_fin")
    st_k = [C("st_ka"), C("st_kb")]
    st_q = [C("st_qa"), C("st_qb")]
    st_v = [C("st_va"), C("st_vb")]
    st_xn = [C("st_xna"), C("st_xnb")]
    st_res = [C("st_resa"), C("st_resb")]
    st_out = [C("st_outa"), C("st_outb")]
    c_t = C("c_t")
    c_v = C("c_v")
    c_s = C("c_s")
    c_qk = [C("c_qkE"), C("c_qkO")]
    c_ex = [C("c_exE"), C("c_exO")]
    c_pv = [C("c_pvE"), C("c_pvO")]
    c_fin = C("c_fin")
    c_e = C("c_e")
    c_fbb = C("c_fbb")
    c_ftt = C("c_ftt")

    r = Rec()
    r.op("sync", "dma_start", out=g_ident[:], in_=ident_bf, inc=s_const.bump(16))
    r.op("sync", "dma_start", out=g_pmat[:], in_=pmat_bf, inc=s_const.bump(16))
    r.op("sync", "dma_start", out=g_ones_bf[:], in_=ones_bf, inc=s_const.bump(16))
    r.op("sync", "dma_start", out=g_ones_f[:], in_=ones_f32, inc=s_const.bump(16))
    r.op("sync", "dma_start", out=g_c64[:], in_=c64_bf, inc=s_const.bump(16))
    r.op("sync", "dma_start", out=g_lnw[:], in_=lnw_d, inc=s_const.bump(16))
    r.op("sync", "dma_start", out=g_lnb[:], in_=lnb_d, inc=s_const.bump(16))
    r.op("gpsimd", "dma_start", out=g_brow[:], in_=bout_d, inc=s_cg.bump(16))
    r.emit(nc)

    es1 = ExitStack()

    def sb1(name, shape, dt):
        return es1.enter_context(nc.sbuf_tensor(name, shape, dt))

    ctab = [[sb1(f"ctab{p}{w}", [128, BLK], F32) for w in range(2)] for p in range(2)]
    lo_sml = [sb1(f"lo_sml{p}", [128, BLK], F32) for p in range(2)]
    wqkv_bf = sb1("wqkv_bf", [128, 8 * 3 * D], BF16)
    x_t = [sb1(f"x_t{i}", [128, D], F32) for i in range(4)]
    xn_tm = [sb1(f"xn_tm{i}", [128, D], BF16) for i in range(4)]
    xn_fm = [sb1(f"xn_fm{i}", [128, 8 * BLK], BF16) for i in range(2)]
    k_blk = [sb1(f"k_blk{i}", [128, NP * BLK], BF16) for i in range(2)]
    q_blk = [sb1(f"q_blk{i}", [128, NP * BLK], BF16) for i in range(2)]
    v_blk = [sb1(f"v_blk{i}", [128, NP * 4 * 256], BF16) for i in range(2)]
    kbf = [sb1(f"kbf{i}", [128, BLK], BF16) for i in range(2)]
    t1_scr = sb1("t1_scr", [128, BLK], F32)
    t2_scr = sb1("t2_scr", [128, BLK], F32)
    sq_scr = sb1("sq_scr", [128, D], F32)
    st_a = sb1("st_a", [128, 4], F32)
    st_b = sb1("st_b", [128, 4], F32)
    st_c = sb1("st_c", [128, 4], F32)
    st_r = sb1("st_r", [128, 4], F32)
    st_nb = sb1("st_nb", [128, 4], F32)

    pK = [es1.enter_context(nc.psum_tensor(f"pK{i}", [128, BLK], F32)) for i in range(2)]
    pR = [es1.enter_context(nc.psum_tensor(f"pR{i}", [128, BLK], F32)) for i in range(2)]
    pV = [es1.enter_context(nc.psum_tensor(f"pV{i}", [128, BLK], F32)) for i in range(2)]
    pT1 = [es1.enter_context(nc.psum_tensor(f"pT1{i}", [128, BLK], BF16)) for i in range(2)]

    r = Rec()
    tab_free = [0, 0]
    tab_ready = [0, 0]

    def load_tables(blk):
        par = blk % 2
        tok0 = blk * BLK
        for w, hi_d, lo_d in ((0, cos_hi_d, cos_lo_d), (1, sin_hi_d, sin_lo_d)):
            r.wait("sync", c_v, tab_free[par])
            r.op("sync", "dma_start", out=ctab[par][w][:],
                 in_=hi_d[:, tok0:tok0 + BLK], inc=s_tab.bump(16))
            r.op("sync", "dma_start", out=lo_sml[par][:],
                 in_=lo_d[:, tok0:tok0 + BLK], inc=s_tab.bump(16))
            r.wait("vector", s_tab, s_tab.v)
            r.op("vector", "tensor_add", ctab[par][w][:], ctab[par][w][:],
                 lo_sml[par][:], inc=c_v.bump())
        tab_ready[par] = c_v.v

    load_tables(0)
    r.op("gpsimd", "dma_start", out=wqkv_bf[:],
         in_=wqkv_d.rearrange("(c p) n -> p c n", p=128), inc=s_wq.bump(16))
    r.wait("tensor", s_const)
    r.wait("vector", s_const)
    r.wait("scalar", s_const)

    wfc_rows_g = wfc_d.rearrange("(c p) d -> c p d", p=128)
    wpj_rows_g = wproj_d.rearrange("(c p) (u w) -> c u p w", p=128, w=1024)
    gamma_jobs = [wfc_rows_g[i] for i in range(32)] + \
                 [wpj_rows_g[ci, ui] for ci in range(8) for ui in range(4)]
    gamma_ct = [0]
    gamma_marks = {}

    def emit_gamma(njobs):
        for _ in range(njobs):
            ch = gamma_ct[0]
            if ch >= len(gamma_jobs):
                return
            bi = ch % 4
            if ch - 4 in gamma_marks:
                r.wait("sync", c_v, gamma_marks[ch - 4])
            r.op("sync", "dma_start", out=wc_buf[bi][:], in_=gamma_jobs[ch],
                 inc=s_w2[bi].bump(16))
            r.wait("vector", s_w2[bi], s_w2[bi].v)
            r.op("vector", "tensor_reduce",
                 red96[:, ch * 4:(ch + 1) * 4].rearrange("p (s u) -> p s u", u=1),
                 wc_buf[bi][:].rearrange("p (s u) -> p s u", u=256),
                 AX, ALU.add, apply_absolute_value=True, inc=c_v.bump())
            gamma_marks[ch] = c_v.v
            gamma_ct[0] += 1

    pk_last, pr_last, kbf_last = {}, {}, {}

    cv_xnfm_mark = [0, 0]

    def emit_lnstats(blk):
        nonlocal_vars = None
        tok0 = blk * BLK
        sxb = s_x.v
        r.wait("sync", c_s, ln_state["mark_xload_s"])
        r.wait("sync", c_v, ln_state["mark_xload_v"])
        for tt in range(4):
            r.op("sync", "dma_start", out=x_t[tt][:],
                 in_=x_d[tok0 + tt * 128: tok0 + (tt + 1) * 128, :], inc=s_x.bump(16))
        if blk >= 1:
            load_tables(blk)
        emit_gamma(8)
        r.wait("scalar", c_v, ln_state["prev_cv_stats"])
        for tt in range(4):
            r.wait("vector", s_x, sxb + 64)
            r.op("vector", "reduce_sum", st_a[:, tt:tt + 1], x_t[tt][:], AX,
                 inc=(c_e.bump() if tt == 3 else c_v.bump()))
            if tt == 3:
                r.wait("vector", c_e)
                r.op("vector", "engine_nop", inc=c_v.bump())
            r.wait("scalar", s_x, sxb + 64)
            r.op("scalar", "activation", sq_scr[:], x_t[tt][:], AF.Square,
                 accum_out=st_b[:, tt:tt + 1], inc=c_s.bump())
        ln_state["mark_xload_v"] = c_v.v
        r.wait("vector", c_s)

        def vedge():
            r.q["vector"][-1] = r.q["vector"][-1][:4] + (c_e.bump(),)
            r.wait("vector", c_e)

        r.op("vector", "tensor_scalar_mul", st_a[:], st_a[:], 1.0 / D)
        vedge()
        r.op("vector", "tensor_scalar_mul", st_b[:], st_b[:], 1.0 / D)
        vedge()
        r.op("vector", "tensor_mul", st_c[:], st_a[:], st_a[:])
        vedge()
        r.op("vector", "tensor_sub", st_b[:], st_b[:], st_c[:])
        vedge()
        r.op("vector", "tensor_scalar_add", st_b[:], st_b[:], EPS_LN, inc=c_v.bump())
        r.wait("scalar", c_v)
        r.op("scalar", "activation", st_c[:], st_b[:], AF.Sqrt, inc=c_s.bump())
        r.wait("vector", c_s)
        r.op("vector", "reciprocal", st_r[:], st_c[:])
        vedge()
        r.op("vector", "tensor_mul", st_c[:], st_r[:], st_r[:])
        vedge()
        r.op("vector", "tensor_mul", st_c[:], st_c[:], st_b[:])
        vedge()
        r.op("vector", "tensor_scalar", st_c[:], st_c[:], -0.5, 1.5, ALU.mult, ALU.add)
        vedge()
        r.op("vector", "tensor_mul", st_r[:], st_r[:], st_c[:])
        vedge()
        r.op("vector", "tensor_mul", st_nb[:], st_a[:], st_r[:])
        vedge()
        r.op("vector", "tensor_scalar_mul", st_nb[:], st_nb[:], -1.0, inc=c_v.bump())
        cv_stats = c_v.v
        ln_state["prev_cv_stats"] = cv_stats
        r.wait("scalar", c_v, cv_stats)
        r.wait("scalar", c_t, ln_state["mark_tp"])
        for tt in range(4):
            r.op("scalar", "activation", xn_tm[tt][:], x_t[tt][:], AF.Identity,
                 bias=st_nb[:, tt:tt + 1], scale=st_r[:, tt:tt + 1], inc=c_s.bump())
        ln_state["mark_xload_s"] = c_s.v

    def emit_transposes(blk):
        bi = blk % 2
        r.wait("tensor", c_s, ln_state["mark_xload_s"])
        r.wait("tensor", st_xn[bi], st_xn[bi].v)
        r.wait("tensor", c_t, ln_state["mark_xnfm_free"][bi])
        r.wait("tensor", c_v)
        for d in range(8):
            if d >= 2:
                r.wait("tensor", c_v)
            for tt in range(4):
                r.op("tensor", "transpose", pT1[d % 2][:, tt * 128:(tt + 1) * 128],
                     xn_tm[tt][:, d * 128:(d + 1) * 128], g_ident[:],
                     inc=(c_t.bump() if tt == 3 else None))
            r.wait("vector", c_t)
            r.op("vector", "tensor_copy", xn_fm[bi][:, d * BLK:(d + 1) * BLK],
                 pT1[d % 2][:], inc=c_v.bump())
        ln_state["mark_tp"] = c_t.v
        cv_xnfm_mark[bi] = c_v.v
        if blk < 4:
            tok0 = blk * BLK
            r.wait("gpsimd", c_v, cv_xnfm_mark[bi])
            r.op("gpsimd", "dma_start", out=xn_s[:, :, tok0:tok0 + BLK],
                 in_=xn_fm[bi][:].rearrange("p (d t) -> p d t", t=BLK),
                 inc=st_xn[bi].bump(16))

    def emit_qkv(blk):
        own_blk = blk < 4
        tok0 = blk * BLK
        bi = blk % 2
        r.wait("tensor", s_wq, 16)
        r.wait("tensor", c_v, cv_xnfm_mark[bi])

        def proj_rope(col0, buf, jbase):
            ct_proj = {}
            cv_kbf = {}
            for p in range(NP + 1):
                if p < NP:
                    j = jbase + p
                    pk = pK[j % 2]
                    if j % 2 in pk_last:
                        r.wait("tensor", c_v, pk_last[j % 2])
                    for d in range(8):
                        r.op("tensor", "matmul", pk[:],
                             wqkv_bf[:, d * 3 * D + col0 + p * 128: d * 3 * D + col0 + (p + 1) * 128],
                             xn_fm[bi][:, d * BLK:(d + 1) * BLK],
                             start=(d == 0), stop=(d == 7),
                             inc=(c_t.bump() if d == 7 else None))
                    ct_proj[p] = c_t.v
                    if j % 2 in kbf_last:
                        r.wait("vector", c_t, kbf_last[j % 2])
                    r.wait("vector", c_t, ct_proj[p])
                    r.op("vector", "tensor_copy", kbf[j % 2][:], pk[:], inc=c_v.bump())
                    cv_kbf[p] = c_v.v
                if p >= 1:
                    pm1 = p - 1
                    jm = jbase + pm1
                    pkm, prm, kbm = pK[jm % 2], pR[jm % 2], kbf[jm % 2]
                    r.wait("tensor", c_v, cv_kbf[pm1])
                    if jm % 2 in pr_last:
                        r.wait("tensor", c_v, pr_last[jm % 2])
                    r.op("tensor", "matmul", prm[:], g_pmat[:], kbm[:],
                         start=True, stop=True, inc=c_t.bump())
                    kbf_last[jm % 2] = c_t.v
                    ct_rot = c_t.v
                    r.op("vector", "tensor_mul", t1_scr[:], pkm[:],
                         ctab[blk % 2][0][:], inc=c_v.bump())
                    pk_last[jm % 2] = c_v.v
                    r.wait("vector", c_t, ct_rot)
                    r.op("vector", "tensor_mul", t2_scr[:], prm[:],
                         ctab[blk % 2][1][:], inc=c_v.bump())
                    pr_last[jm % 2] = c_v.v
                    r.op("vector", "tensor_add", buf[:, pm1 * BLK:(pm1 + 1) * BLK],
                         t1_scr[:], t2_scr[:], inc=c_v.bump())

        r.wait("vector", st_k[bi], st_k[bi].v)
        proj_rope(D, k_blk[bi], blk * 24)
        r.wait("gpsimd", c_v)
        r.op("gpsimd", "dma_start", out=k_s[:, :, tok0:tok0 + BLK],
             in_=k_blk[bi][:].rearrange("p (n t) -> p n t", t=BLK),
             inc=st_k[bi].bump(16))
        if own_blk:
            r.wait("vector", st_q[bi], st_q[bi].v)
            proj_rope(0, q_blk[bi], blk * 24 + 8)
            r.wait("gpsimd", c_v)
            r.op("gpsimd", "dma_start", out=q_s[:, :, tok0:tok0 + BLK],
                 in_=q_blk[bi][:].rearrange("p (n t) -> p n t", t=BLK),
                 inc=st_q[bi].bump(16))
        vb = v_blk[bi]
        r.wait("vector", st_v[bi], st_v[bi].v)
        if blk < 2:
            r.op("vector", "memset",
                 vb[:].rearrange("p (k c) -> p k c", c=256)[:, :, 64:192], 1.0,
                 inc=c_v.bump())
        r.wait("tensor", c_v)
        for tt in range(4):
            for cc in range(2):
                jv = tt * 2 + cc
                pv = pV[jv % 2]
                if jv >= 2:
                    r.wait("tensor", c_v)
                for d in range(8):
                    r.op("tensor", "matmul", pv[:],
                         xn_fm[bi][:, d * BLK + tt * 128: d * BLK + (tt + 1) * 128],
                         wqkv_bf[:, d * 3 * D + 2 * D + cc * 512: d * 3 * D + 2 * D + (cc + 1) * 512],
                         start=(d == 0), stop=(d == 7),
                         inc=(c_t.bump() if d == 7 else None))
                r.wait("vector", c_t)
                srcv = pv[:].rearrange("p (n j c) -> p n j c", j=2, c=64)
                vbv = vb[:].rearrange("p (n k c) -> p n k c", k=4, c=256)
                r.op("vector", "tensor_copy",
                     vbv[:, cc * 4:(cc + 1) * 4, tt, 0:64], srcv[:, :, 0, :])
                r.op("vector", "tensor_copy",
                     vbv[:, cc * 4:(cc + 1) * 4, tt, 192:256], srcv[:, :, 1, :],
                     inc=c_v.bump())
        ln_state["mark_xnfm_free"][bi] = c_t.v
        tab_free[blk % 2] = c_v.v
        r.wait("gpsimd", c_v)
        r.op("gpsimd", "dma_start",
             out=v_s[:, :, blk * 4 * 256:(blk + 1) * 4 * 256],
             in_=vb[:].rearrange("p (n c) -> p n c", c=4 * 256),
             inc=st_v[bi].bump(16))

    ln_state = {"mark_xload_s": 0, "mark_xload_v": c_v.v, "prev_cv_stats": 0,
                "mark_xnfm_free": [0, 0], "mark_tp": 0}
    emit_lnstats(0)
    emit_transposes(0)
    for blk in range(NBLK):
        if blk + 1 < NBLK:
            emit_lnstats(blk + 1)
        emit_qkv(blk)
        if blk + 1 < NBLK:
            emit_transposes(blk + 1)
    emit_gamma(len(gamma_jobs))

    def finish_thr(cols, thr):
        r.op("vector", "tensor_reduce", acc1[:], red96[:, cols[0] * 4:cols[1] * 4],
             AX, ALU.add, inc=c_e.bump())
        r.wait("vector", c_e)
        r.op("vector", "engine_nop", inc=c_v.bump())
        r.wait("tensor", c_v)
        for par in (0, 1):
            if par in pr_last:
                r.wait("tensor", c_v, pr_last[par])
        r.op("tensor", "matmul", pR[0][0:1, 0:1], g_ones_f[:, 0:1], acc1[:],
             start=True, stop=True, inc=c_t.bump())
        r.wait("vector", c_t)
        r.op("vector", "tensor_scalar", thr11[:], pR[0][0:1, 0:1], 0.5 / (DFF * D),
             0.5e-6, ALU.mult, ALU.add, inc=c_v.bump())
        r.wait("tensor", c_v)
        r.op("tensor", "matmul", pR[0][:, 0:1], g_ones_f[0:1, 0:128], thr11[:],
             start=True, stop=True, inc=c_t.bump())
        r.wait("vector", c_t)
        r.op("vector", "tensor_copy", thr[:, 0:1], pR[0][:, 0:1])
        r.op("vector", "tensor_scalar_mul", thr[:, 1:2], pR[0][:, 0:1], -1.0,
             inc=c_v.bump())

    finish_thr((0, 32), thr_fc)
    finish_thr((32, 64), thr_pj)
    r.op("vector", "engine_nop", inc=c_e.bump())
    r.wait("vector", c_e)
    r.op("vector", "engine_nop", inc=c_v.bump())
    cv_thr = c_v.v

    p1_end = dict(c_v=c_v.v, c_s=c_s.v, c_t=c_t.v)
    p1_stores = [(st_k[0], st_k[0].v), (st_k[1], st_k[1].v),
                 (st_q[0], st_q[0].v), (st_q[1], st_q[1].v),
                 (st_v[0], st_v[0].v), (st_v[1], st_v[1].v),
                 (st_xn[0], st_xn[0].v), (st_xn[1], st_xn[1].v)]
    r.emit(nc)
    es1.close()

    es2 = ExitStack()

    def sb2(name, shape, dt):
        return es2.enter_context(nc.sbuf_tensor(name, shape, dt))

    wqfcT = sb2("wqfcT", [128, 8 * DFF], BF16)
    wqprojT = sb2("wqprojT", [128, 32 * D], BF16)
    h_sb = sb2("h_sb", [128, 32 * TT2], BF16)
    xn_t = [sb2(f"xn_t{i}", [128, 8 * TT2], BF16) for i in range(2)]
    mu_sb = sb2("mu_sb", [128, TT2], F32)
    ex2_sb = sb2("ex2_sb", [128, TT2], F32)
    var_sb = sb2("var_sb", [128, TT2], F32)
    rr_sb = sb2("rr_sb", [128, TT2], F32)
    tln_scr = sb2("tln_scr", [128, TT2], F32)
    x2_t = [sb2(f"x2_t{i}", [128, D], F32) for i in range(2)]
    out2_t = [sb2(f"out2_t{i}", [128, D], F32) for i in range(2)]

    es2a = ExitStack()
    pT2 = [es2a.enter_context(nc.psum_tensor(f"pT2{i}", [128, 512], BF16)) for i in range(2)]

    r = Rec()
    for e in ENGINES:
        for ctr, vv in p1_stores:
            r.wait(e, ctr, vv)
        r.wait(e, c_v, p1_end["c_v"])
        r.wait(e, c_s, p1_end["c_s"])
        r.wait(e, c_t, p1_end["c_t"])

    wfc_rows = wfc_d.rearrange("(c p) d -> c p d", p=128)
    wpj_rows = wproj_d.rearrange("(c p) n -> c p n", p=128)

    tq_marks = {}
    tqt_marks = {}

    def quant_transpose(load_ap, thr, ntiles, dst_fn, qi):
        bi = qi % 2
        w = load_ap.shape[-1]
        if qi - 2 in tq_marks:
            r.wait("sync", c_v, tq_marks[qi - 2])
        else:
            r.wait("sync", c_v, cv_thr)
        r.op("sync", "dma_start", out=wc_buf[bi][:, 0:w], in_=load_ap,
             inc=s_w2[bi].bump(16))
        r.wait("vector", s_w2[bi], s_w2[bi].v)
        r.wait("vector", c_v, cv_thr)
        if qi - 2 in tqt_marks:
            r.wait("vector", c_t, tqt_marks[qi - 2])
        wv = wc_buf[bi][:, 0:w]
        qn = wqn_buf[bi][:, 0:w]
        r.op("vector", "tensor_single_scalar", qn, wv, thr[:, 1:2], ALU.is_lt)
        r.op("vector", "scalar_tensor_tensor", qn, wv, thr[:, 0:1], qn,
             ALU.is_gt, ALU.subtract, inc=c_v.bump())
        tq_marks[qi] = c_v.v
        cv_q = c_v.v
        evac_marks = {}
        for grp in range(ntiles // 4):
            pt = pT2[grp % 2]
            r.wait("tensor", c_v, evac_marks.get(grp - 2, cv_q))
            for u in range(4):
                ti = grp * 4 + u
                r.op("tensor", "transpose", pt[:, u * 128:(u + 1) * 128],
                     qn[:, ti * 128:(ti + 1) * 128], g_ident[:],
                     inc=(c_t.bump() if u == 3 else None))
            r.wait("vector", c_t)
            r.op("vector", "tensor_copy", dst_fn(grp), pt[:].rearrange(
                "p (u c) -> p u c", c=128), inc=c_v.bump())
            evac_marks[grp] = c_v.v
        tqt_marks[qi] = c_t.v

    qi = 0
    wqfcT_v = wqfcT[:].rearrange("p (d h) -> p d h", h=DFF)
    for c in range(32):
        def dst_fc(grp, c=c):
            return wqfcT_v[:, 4 * grp:4 * grp + 4, c * 128:(c + 1) * 128]
        quant_transpose(wfc_rows[c], thr_fc, 8, dst_fc, qi)
        qi += 1
    wqprojT_v = wqprojT[:].rearrange("p (m d) -> p m d", d=D)
    for ci in range(8):
        for ui in range(4):
            def dst_pj(grp, ci=ci, ui=ui):
                return wqprojT_v[:, 8 * ui + 4 * grp: 8 * ui + 4 * grp + 4,
                                 ci * 128:(ci + 1) * 128]
            quant_transpose(wpj_rows[ci][:, ui * 1024:(ui + 1) * 1024], thr_pj, 8,
                            dst_pj, qi)
            qi += 1

    cv_wq_done = c_v.v
    if globals().get("_DBG_WQ"):
        _k = "ExternalOutput" if globals().get("_DBG_IO") else "Internal"
        dbg_fc = nc.dram_tensor("dbg_fc", [128, 8 * DFF], BF16, kind=_k).ap()
        dbg_pj = nc.dram_tensor("dbg_pj", [128, 32 * D], BF16, kind=_k).ap()
        r.wait("gpsimd", c_v, cv_wq_done)
        r.op("gpsimd", "dma_start", out=dbg_fc, in_=wqfcT[:], inc=st_k[0].bump(16))
        r.op("gpsimd", "dma_start", out=dbg_pj, in_=wqprojT[:], inc=st_k[0].bump(16))
    es2a.close()
    h2_ring = sb2("h2_ring", [128, 4 * TT2], BF16)
    pF = [es2.enter_context(nc.psum_tensor(f"pF{i}", [128, TT2], F32)) for i in range(2)]
    pS1 = es2.enter_context(nc.psum_tensor("pS1", [128, TT2], F32))
    pS2 = es2.enter_context(nc.psum_tensor("pS2", [128, TT2], F32))
    pP2 = [es2.enter_context(nc.psum_tensor(f"pP2{i}", [128, D], F32)) for i in range(2)]

    xn_marks = {}
    res_marks = {}
    for ti in range(OWN // TT2):
        bi = ti % 2
        if ti - 2 in xn_marks:
            r.wait("sync", c_t, xn_marks[ti - 2])
        r.op("sync", "dma_start", out=xn_t[bi][:],
             in_=xn_s[:, :, ti * TT2:(ti + 1) * TT2], inc=s_xn2[bi].bump(16))
        r.wait("tensor", s_xn2[bi], s_xn2[bi].v)
        r.wait("tensor", c_v, cv_wq_done)
        gelu_marks = {}
        h2_done = None
        for m in range(32):
            pf = pF[m % 2]
            if m - 2 in gelu_marks:
                r.wait("tensor", c_s, gelu_marks[m - 2])
            for d in range(8):
                r.op("tensor", "matmul", pf[:],
                     wqfcT[:, d * DFF + m * 128: d * DFF + (m + 1) * 128],
                     xn_t[bi][:, d * TT2:(d + 1) * TT2],
                     start=(d == 0), stop=(d == 7),
                     inc=(c_t.bump() if d == 7 else None))
            r.wait("scalar", c_t)
            r.op("scalar", "activation", h_sb[:, m * TT2:(m + 1) * TT2], pf[:],
                 AF.Gelu, inc=c_s.bump())
            gelu_marks[m] = c_s.v
        cs_gelu_all = c_s.v
        r.wait("tensor", c_s, cs_gelu_all)
        for m in range(32):
            r.op("tensor", "matmul", pS1[:], g_ones_bf[:],
                 h_sb[:, m * TT2:(m + 1) * TT2], start=(m == 0), stop=(m == 31),
                 skip_group_check=True, inc=(c_t.bump() if m == 31 else None))
        h2_marks = {}
        ps2_marks = {}
        for m in range(32):
            sl = m % 4
            r.wait("vector", c_s, gelu_marks[m])
            if m - 4 in ps2_marks:
                r.wait("vector", c_t, ps2_marks[m - 4])
            r.op("vector", "tensor_mul",
                 h2_ring[:, sl * TT2:(sl + 1) * TT2], h_sb[:, m * TT2:(m + 1) * TT2],
                 h_sb[:, m * TT2:(m + 1) * TT2], inc=c_v.bump())
            h2_marks[m] = c_v.v
            r.wait("tensor", c_v, h2_marks[m])
            r.op("tensor", "matmul", pS2[:], g_ones_bf[:],
                 h2_ring[:, sl * TT2:(sl + 1) * TT2], start=(m == 0), stop=(m == 31),
                 skip_group_check=True, inc=c_t.bump())
            ps2_marks[m] = c_t.v
        xn_marks[ti] = c_t.v
        ct_stats = c_t.v
        r.wait("vector", c_t, ct_stats)
        r.wait("vector", c_s)
        r.op("vector", "tensor_scalar_mul", mu_sb[:], pS1[:], 1.0 / DFF)
        r.op("vector", "tensor_scalar_mul", ex2_sb[:], pS2[:], 1.0 / DFF)
        r.op("vector", "tensor_mul", var_sb[:], mu_sb[:], mu_sb[:])
        r.op("vector", "tensor_sub", var_sb[:], ex2_sb[:], var_sb[:])
        r.op("vector", "tensor_scalar_add", var_sb[:], var_sb[:], EPS_LN, inc=c_v.bump())
        r.wait("scalar", c_v)
        r.op("scalar", "activation", ex2_sb[:], var_sb[:], AF.Sqrt, inc=c_s.bump())
        r.wait("vector", c_s)
        r.op("vector", "reciprocal", rr_sb[:], ex2_sb[:])
        r.op("vector", "tensor_mul", tln_scr[:], rr_sb[:], rr_sb[:])
        r.op("vector", "tensor_mul", tln_scr[:], tln_scr[:], var_sb[:])
        r.op("vector", "tensor_scalar", tln_scr[:], tln_scr[:], -0.5, 1.5, ALU.mult, ALU.add)
        r.op("vector", "tensor_mul", rr_sb[:], rr_sb[:], tln_scr[:], inc=c_v.bump())
        cv_r_done = c_v.v
        for tm in range(TT2 // 128):
            sub = ti * (TT2 // 128) + tm
            bi2 = tm
            if (ti - 1, tm) in res_marks:
                r.wait("sync", c_v, res_marks[(ti - 1, tm)])
            row0 = ti * TT2 + tm * 128
            r.op("sync", "dma_start", out=x2_t[bi2][:], in_=x_d[row0:row0 + 128, :],
                 inc=s_res[bi2].bump(16))
        for m in range(32):
            r.op("vector", "tensor_sub", tln_scr[:], h_sb[:, m * TT2:(m + 1) * TT2], mu_sb[:])
            r.op("vector", "tensor_mul", tln_scr[:], tln_scr[:], rr_sb[:])
            r.op("vector", "tensor_scalar", h_sb[:, m * TT2:(m + 1) * TT2], tln_scr[:],
                 g_lnw[:, m:m + 1], g_lnb[:, m:m + 1], ALU.mult, ALU.add,
                 inc=c_v.bump())
            cv_norm_m = c_v.v
            if m == 0:
                for tm in range(TT2 // 128):
                    if (ti - 1, tm) in res_marks:
                        r.wait("tensor", c_v, res_marks[(ti - 1, tm)])
            r.wait("tensor", c_v, cv_norm_m)
            for tm in range(TT2 // 128):
                for n in range(2):
                    r.op("tensor", "matmul", pP2[tm][:, n * 512:(n + 1) * 512],
                         h_sb[:, m * TT2 + tm * 128: m * TT2 + (tm + 1) * 128],
                         wqprojT[:, m * D + n * 512: m * D + (n + 1) * 512],
                         start=(m == 0), stop=(m == 31), skip_group_check=True,
                         inc=(c_t.bump() if (m == 31 and n == 1 and tm == TT2 // 128 - 1)
                              else None))
        ct_proj = c_t.v
        for tm in range(TT2 // 128):
            sub = ti * (TT2 // 128) + tm
            bi2 = tm
            row0 = ti * TT2 + tm * 128
            r.wait("vector", c_t, ct_proj)
            r.wait("vector", s_res[bi2], s_res[bi2].v)
            r.wait("vector", st_res[bi2], st_res[bi2].v)
            r.op("vector", "tensor_add", out2_t[bi2][:], pP2[tm][:], x2_t[bi2][:],
                 inc=c_v.bump())
            res_marks[(ti, tm)] = c_v.v
            r.wait("gpsimd", c_v, res_marks[(ti, tm)])
            r.op("gpsimd", "dma_start", out=out_d[row0:row0 + 128, :],
                 in_=out2_t[bi2][:], inc=st_res[bi2].bump(16))

    p2_end = dict(c_v=c_v.v, c_s=c_s.v, c_t=c_t.v)
    p2_stores = [(st_res[0], st_res[0].v), (st_res[1], st_res[1].v)]
    r.emit(nc)
    es2.close()

    es34 = ExitStack()
    attn_all = [es34.enter_context(nc.sbuf_tensor(f"attn_all{i}", [128, OWN], BF16))
                for i in range(NP)]
    es3 = ExitStack()

    def sb3(name, shape, dt):
        return es3.enter_context(nc.sbuf_tensor(name, shape, dt))

    kb = [sb3(f"kb{i}", [128, T], BF16) for i in range(2)]
    qb = [sb3(f"qb{i}", [128, OWN], BF16) for i in range(2)]
    kb2 = [sb3(f"kb2{i}", [128, T], BF16) for i in range(2)]
    qb2 = [sb3(f"qb2{i}", [128, OWN], BF16) for i in range(2)]
    vb3 = [sb3(f"vb3{i}", [128, 32 * 256], BF16) for i in range(2)]
    probs = [[sb3(f"probs{s}{i}", [128, 1024], BF16) for i in range(4)] for s in range(1)]
    recip_sb = sb3("recip_sb", [128, 1024], BF16)
    bb_sb = sb3("bb_sb", [128, 1024], F32)
    pvsum_sb = sb3("pvsum_sb", [128, 1024], F32)

    pQK = [es3.enter_context(nc.psum_tensor(f"pQK{i}", [128, 1024], F32)) for i in range(2)]
    pPVa = es3.enter_context(nc.psum_tensor("pPVa", [128, 1024], F32))
    pPVb = es3.enter_context(nc.psum_tensor("pPVb", [128, 1024], F32))

    r = Rec()
    for e in ENGINES:
        for ctr, vv in p2_stores:
            r.wait(e, ctr, vv)
        r.wait(e, c_v, p2_end["c_v"])
        r.wait(e, c_s, p2_end["c_s"])
        r.wait(e, c_t, p2_end["c_t"])

    fin_mark = {}
    kvq_mark = {}
    fin_count = 0
    for pi in range(NP):
        bib = pi % 2
        if pi - 2 in fin_mark:
            r.wait("sync", c_pv[0], fin_mark[pi - 2])
        r.op("sync", "dma_start", out=kb[bib][:], in_=k_s[:, pi, :],
             inc=s_kvq[bib].bump(16))
        r.op("sync", "dma_start", out=qb[bib][:], in_=q_s[:, pi, :],
             inc=s_kvq[bib].bump(16))
        r.op("sync", "dma_start", out=vb3[bib][:], in_=v_s[:, pi, :],
             inc=s_kvq[bib].bump(16))
        r.op("sync", "dma_start", out=kb2[bib][64:128, :], in_=k_s[0:64, pi, :],
             inc=s_kvq[bib].bump(16))
        r.op("sync", "dma_start", out=kb2[bib][0:64, :], in_=k_s[64:128, pi, :],
             inc=s_kvq[bib].bump(16))
        r.op("sync", "dma_start", out=qb2[bib][64:128, :], in_=q_s[0:64, pi, :],
             inc=s_kvq[bib].bump(16))
        r.op("sync", "dma_start", out=qb2[bib][0:64, :], in_=q_s[64:128, pi, :],
             inc=s_kvq[bib].bump(16))
        kvq_mark[pi] = s_kvq[bib].v
        for qc in range(2):
            for side in range(2):
                r0 = 64 * side
                d0, d1 = (64, 128) if side == 0 else (0, 64)
                v0, v1 = (0, 64) if side == 0 else (64, 128)
                e0, q0, p0 = c_ex[0].v, c_qk[0].v, c_pv[0].v

                def emit_pv(ktp, bib=bib, side=side, e0=e0, p0=p0):
                    if ktp == 0:
                        r.wait("tensor", c_ftt)
                    r.wait("tensor", c_ex[0], e0 + ktp + 1)
                    kc = ktp * 256 + side * 128
                    for n in range(2):
                        r.op("tensor", "matmul", pPVa[:, n * 512:(n + 1) * 512],
                             vb3[bib][0:64, kc:kc + 128],
                             probs[0][ktp % 4][0:64, n * 512:(n + 1) * 512],
                             start=(ktp == 0), stop=(ktp == 31), skip_group_check=True,
                             tile_position=(0, 0))
                        r.op("tensor", "matmul", pPVb[:, n * 512:(n + 1) * 512],
                             vb3[bib][64:128, kc:kc + 128],
                             probs[0][ktp % 4][64:128, n * 512:(n + 1) * 512],
                             start=(ktp == 0), stop=(ktp == 31), skip_group_check=True,
                             tile_position=(64, 0),
                             inc=(c_pv[0].bump() if n == 1 else None))

                for s in range(16):
                    kt0, kt1 = 2 * s, 2 * s + 1
                    if s == 0:
                        r.wait("tensor", s_kvq[bib], kvq_mark[pi])
                    r.wait("tensor", c_ex[0], e0 + kt0)
                    for kt in (kt0, kt1):
                        pq = pQK[kt % 2]
                        if kt % 2 == 0:
                            ksrc, qsrc, rr0 = kb[bib], qb[bib], r0
                        else:
                            ksrc, qsrc, rr0 = kb2[bib], qb2[bib], 64 - r0
                        for n in range(2):
                            r.op("tensor", "matmul", pq[:, n * 512:(n + 1) * 512],
                                 ksrc[rr0:rr0 + 64, kt * 128:(kt + 1) * 128],
                                 qsrc[rr0:rr0 + 64, qc * 1024 + n * 512: qc * 1024 + (n + 1) * 512],
                                 start=True, stop=True, tile_position=(rr0, 0),
                                 inc=(c_qk[0].bump() if n == 1 else None))
                    for kt in (kt0, kt1):
                        r.wait("scalar", c_qk[0], q0 + kt + 1)
                        r.wait("scalar", c_pv[0], p0 + kt - 3)
                        r.op("scalar", "activation", probs[0][kt % 4][:], pQK[kt % 2][:],
                             AF.Exp, scale=0.125, inc=c_ex[0].bump())
                    if s >= 1:
                        emit_pv(kt0 - 2)
                        emit_pv(kt1 - 2)
                emit_pv(30)
                emit_pv(31)
                r.wait("vector", c_pv[0])
                r.op("vector", "tensor_copy", pvsum_sb[:], pPVa[:])
                r.op("vector", "tensor_add", pvsum_sb[:], pvsum_sb[:], pPVb[:],
                     inc=c_ftt.bump())
                r.op("vector", "reciprocal", recip_sb[d0:d1, :], pvsum_sb[d0:d1, :],
                     inc=c_v.bump())
                r.wait("sync", c_v)
                r.op("sync", "dma_start", out=recip_sb[v0:v1, :],
                     in_=recip_sb[d0:d1, :], inc=s_fin.bump(16))
                r.wait("vector", s_fin, s_fin.v)
                r.op("vector", "tensor_mul",
                     attn_all[pi][v0:v1, qc * 1024:(qc + 1) * 1024],
                     pvsum_sb[v0:v1, :], recip_sb[v0:v1, :], inc=c_v.bump())
                fin_count += 1
        fin_mark[pi] = c_pv[0].v

    p3_end = dict(c_fin=c_fin.v, c_v=c_v.v, c_t=c_t.v)
    r.emit(nc)
    es3.close()

    es4 = ExitStack()

    def sb4(name, shape, dt):
        return es4.enter_context(nc.sbuf_tensor(name, shape, dt))

    wout_bf = sb4("wout_bf", [128, 8 * D], BF16)
    res_t = [sb4(f"res_t{i}", [128, D], F32) for i in range(2)]
    out4_t = [sb4(f"out4_t{i}", [128, D], F32) for i in range(2)]
    pP4 = [es4.enter_context(nc.psum_tensor(f"pP4{i}", [128, D], F32)) for i in range(2)]

    r = Rec()
    for e in ENGINES:
        r.wait(e, c_fin, p3_end["c_fin"])
        r.wait(e, c_v, p3_end["c_v"])
        r.wait(e, c_t, p3_end["c_t"])
    r.op("gpsimd", "dma_start", out=wout_bf[:],
         in_=wout_d.rearrange("(r p) n -> p r n", p=128), inc=s_w4.bump(16))
    out_marks = {}
    for ctr, vv in p2_stores:
        r.wait("sync", ctr, vv)
    for tm in range(OWN // 128):
        bi4 = tm % 2
        if tm - 2 in out_marks:
            r.wait("sync", c_v, out_marks[tm - 2])
        r.op("sync", "dma_start", out=res_t[bi4][:],
             in_=out_d[tm * 128:(tm + 1) * 128, :], inc=s_res[bi4].bump(16))
        r.wait("tensor", s_w4, 16)
        r.wait("tensor", s_const)
        r.wait("tensor", s_cg, 16)
        if tm - 2 in out_marks:
            r.wait("tensor", c_v, out_marks[tm - 2])
        for n in range(2):
            for pi in range(NP):
                r.op("tensor", "matmul", pP4[bi4][:, n * 512:(n + 1) * 512],
                     attn_all[pi][:, tm * 128:(tm + 1) * 128],
                     wout_bf[:, pi * D + n * 512: pi * D + (n + 1) * 512],
                     start=(pi == 0), stop=False)
            r.op("tensor", "matmul", pP4[bi4][:, n * 512:(n + 1) * 512],
                 g_ones_bf[0:1, 0:128], g_brow[0:1, n * 512:(n + 1) * 512],
                 start=False, stop=True,
                 inc=(c_t.bump() if n == 1 else None))
        r.wait("vector", c_t)
        r.wait("vector", s_res[bi4], s_res[bi4].v)
        r.wait("vector", st_out[bi4], st_out[bi4].v)
        r.op("vector", "tensor_add", out4_t[bi4][:], pP4[bi4][:], res_t[bi4][:],
             inc=c_v.bump())
        out_marks[tm] = c_v.v
        r.wait("gpsimd", c_v, out_marks[tm])
        r.op("gpsimd", "dma_start", out=out_d[tm * 128:(tm + 1) * 128, :],
             in_=out4_t[bi4][:], inc=st_out[bi4].bump(16))
    r.wait("gpsimd", st_out[0], st_out[0].v)
    r.wait("gpsimd", st_out[1], st_out[1].v)
    r.emit(nc)
    es4.close()
    es34.close()
    ges.close()
    return nc


_NC_CACHE = [None]


def kernel(L0_attn_x, L0_attn_w_qkv, L0_attn_w_out, L0_attn_b_out,
           L0_ffn_W_fc, L0_ffn_ln_w, L0_ffn_ln_b, L0_ffn_W_proj):
    x = np.ascontiguousarray(np.asarray(L0_attn_x, dtype=np.float32))
    wqkv = np.ascontiguousarray(np.asarray(L0_attn_w_qkv, dtype=np.float32))
    wout = np.ascontiguousarray(np.asarray(L0_attn_w_out, dtype=np.float32))
    bout = np.ascontiguousarray(np.asarray(L0_attn_b_out, dtype=np.float32)).reshape(1, D)
    wfc = np.ascontiguousarray(np.asarray(L0_ffn_W_fc, dtype=np.float32))
    lnw = np.ascontiguousarray(np.asarray(L0_ffn_ln_w, dtype=np.float32).reshape(32, 128).T)
    lnb = np.ascontiguousarray(np.asarray(L0_ffn_ln_b, dtype=np.float32).reshape(32, 128).T)
    wproj = np.ascontiguousarray(np.asarray(L0_ffn_W_proj, dtype=np.float32))

    theta = 1.0 / (10000.0 ** (np.arange(0, DH, 2, dtype=np.float64) / DH))
    ang = np.concatenate([np.outer(np.arange(T, dtype=np.float64), theta)] * 2, axis=1)
    cosT = np.cos(ang).T
    sinT = np.sin(ang).T
    cos128 = np.concatenate([cosT, cosT], axis=0)
    sin128 = np.concatenate([sinT, sinT], axis=0)
    cos_hi = cos128.astype(np.float32)
    cos_lo = (cos128 - cos_hi).astype(np.float32)
    sin_hi = sin128.astype(np.float32)
    sin_lo = (sin128 - sin_hi).astype(np.float32)

    if _NC_CACHE[0] is None:
        _NC_CACHE[0] = build_program()
    nc = _NC_CACHE[0]

    in_maps = []
    for c in range(8):
        b, hh = c // 2, c % 2
        sh = hh * OWN
        in_maps.append({
            "x": np.ascontiguousarray(np.roll(x[b], -sh, axis=0)),
            "wqkv": wqkv, "wout": wout, "bout": bout, "wfc": wfc,
            "lnw": lnw, "lnb": lnb, "wproj": wproj,
            "cos_hi": np.ascontiguousarray(np.roll(cos_hi, -sh, axis=1)),
            "cos_lo": np.ascontiguousarray(np.roll(cos_lo, -sh, axis=1)),
            "sin_hi": np.ascontiguousarray(np.roll(sin_hi, -sh, axis=1)),
            "sin_lo": np.ascontiguousarray(np.roll(sin_lo, -sh, axis=1)),
        })

    res = run_bass_kernel_spmd(nc, in_maps, core_ids=list(range(8)))
    out = np.empty((B, T, D), dtype=np.float32)
    for c in range(8):
        b, hh = c // 2, c % 2
        out[b, hh * OWN:(hh + 1) * OWN, :] = res.results[c]["out"]
    return out


# version 17
# speedup vs baseline: 1.0055x; 1.0055x over previous
import numpy as np
import ml_dtypes
from contextlib import ExitStack
import concourse.bass as bass
import concourse.mybir as mybir
from concourse.bass_utils import run_bass_kernel_spmd

F32 = mybir.dt.float32
BF16 = mybir.dt.bfloat16
AF = mybir.ActivationFunctionType
ALU = mybir.AluOpType
AX = mybir.AxisListType.X

B, T, D = 4, 4096, 1024
H, DH = 16, 64
NP = 8
OWN = 2048
NBLK, BLK = 8, 512
TT2 = 256
DFF = 4096
EPS_LN = 1e-5

ENGINES = ("sync", "vector", "scalar", "tensor", "gpsimd")


class Ctr:
    def __init__(self, nc, name):
        self.h = nc.alloc_semaphore(name)
        self.v = 0

    def bump(self, n=1):
        self.v += n
        return (self.h, n)


class Rec:

    def __init__(self):
        self.q = {e: [] for e in ENGINES}

    def op(self, eng, meth, *a, inc=None, **kw):
        self.q[eng].append(("op", meth, a, kw, inc))

    def vedge(self, ctr):
        assert self.q["vector"][-1][4] is None, "vedge would clobber an inc"
        self.q["vector"][-1] = self.q["vector"][-1][:4] + (ctr.bump(),)
        self.wait("vector", ctr)

    def wait(self, eng, ctr, val=None):
        v = ctr.v if val is None else val
        if v > 0:
            self.q[eng].append(("wait", ctr.h, v))

    def emit(self, nc):
        with nc.allow_low_precision(reason="bf16 matmul operands with fp32 accumulation (sanctioned)"), \
             nc.Block() as block:
            for ename in ENGINES:
                items = self.q[ename]
                if not items:
                    continue

                def mk(items=items):
                    def body(e):
                        for it in items:
                            if it[0] == "wait":
                                e.wait_ge(it[1], it[2])
                            else:
                                _, meth, a, kw, inc = it
                                rr = getattr(e, meth)(*a, **kw)
                                if inc is not None:
                                    rr.then_inc(inc[0], inc[1])
                    return body

                getattr(block, ename)(mk())


def build_program():
    nc = bass.Bass("TRN2", target_bir_lowering=False, detect_race_conditions=not globals().get("_RACECHECK", False) and False or globals().get("_RACECHECK", False))

    x_d = nc.dram_tensor("x", [T, D], F32, kind="ExternalInput").ap()
    wqkv_d = nc.dram_tensor("wqkv", [D, 3 * D], F32, kind="ExternalInput").ap()
    wout_d = nc.dram_tensor("wout", [D, D], F32, kind="ExternalInput").ap()
    bout_d = nc.dram_tensor("bout", [1, D], F32, kind="ExternalInput").ap()
    wfc_d = nc.dram_tensor("wfc", [DFF, D], F32, kind="ExternalInput").ap()
    lnw_d = nc.dram_tensor("lnw", [128, 32], F32, kind="ExternalInput").ap()
    lnb_d = nc.dram_tensor("lnb", [128, 32], F32, kind="ExternalInput").ap()
    wproj_d = nc.dram_tensor("wproj", [D, DFF], F32, kind="ExternalInput").ap()
    cos_hi_d = nc.dram_tensor("cos_hi", [128, T], F32, kind="ExternalInput").ap()
    cos_lo_d = nc.dram_tensor("cos_lo", [128, T], F32, kind="ExternalInput").ap()
    sin_hi_d = nc.dram_tensor("sin_hi", [128, T], F32, kind="ExternalInput").ap()
    sin_lo_d = nc.dram_tensor("sin_lo", [128, T], F32, kind="ExternalInput").ap()
    out_d = nc.dram_tensor("out", [OWN, D], F32, kind="ExternalOutput").ap()

    _scratch_kind = "ExternalOutput" if globals().get("_DBG_IO") else "Internal"
    xn_s = nc.dram_tensor("xn_s", [128, 8, OWN], BF16, kind=_scratch_kind).ap()
    k_s = nc.dram_tensor("k_s", [128, NP, T], BF16, kind=_scratch_kind).ap()
    q_s = nc.dram_tensor("q_s", [128, NP, OWN], BF16, kind=_scratch_kind).ap()
    v_s = nc.dram_tensor("v_s", [128, NP, 32 * 256], BF16, kind=_scratch_kind).ap()

    ident_np = np.eye(128, dtype=ml_dtypes.bfloat16)
    pm = np.zeros((128, 128), np.float32)
    for base in (0, 64):
        for m in range(32):
            pm[base + m + 32, base + m] = -1.0
            pm[base + m, base + m + 32] = 1.0
    ident_bf = nc.inline_tensor(ident_np, "c_ident").ap()
    pmat_bf = nc.inline_tensor(pm.astype(ml_dtypes.bfloat16), "c_pmat").ap()
    ones_bf = nc.inline_tensor(np.ones((128, 128), dtype=ml_dtypes.bfloat16), "c_ones").ap()
    ones_f32 = nc.inline_tensor(np.ones((128, 128), dtype=np.float32), "c_onesf").ap()
    c64_bf = nc.inline_tensor(np.full((128, 128), 1.0 / 64, dtype=ml_dtypes.bfloat16), "c_c64").ap()

    ges = ExitStack()

    def gsb(name, shape, dt):
        return ges.enter_context(nc.sbuf_tensor(name, shape, dt))

    g_ident = gsb("g_ident", [128, 128], BF16)
    g_pmat = gsb("g_pmat", [128, 128], BF16)
    g_ones_bf = gsb("g_ones_bf", [128, 128], BF16)
    g_ones_f = gsb("g_ones_f", [128, 128], F32)
    g_c64 = gsb("g_c64", [128, 128], BF16)
    g_lnw = gsb("g_lnw", [128, 32], F32)
    g_lnb = gsb("g_lnb", [128, 32], F32)
    g_brow = gsb("g_brow", [1, D], BF16)
    wc_buf = [gsb(f"wc_buf{i}", [128, 1024], F32) for i in range(4)]
    wqn_buf = [gsb(f"wqn_buf{i}", [128, 1024], BF16) for i in range(2)]
    red96 = gsb("red96", [128, 384], F32)
    acc1 = gsb("acc1", [128, 1], F32)
    thr_fc = gsb("thr_fc", [128, 2], F32)
    thr_pj = gsb("thr_pj", [128, 2], F32)
    thr11 = gsb("thr11", [1, 1], F32)

    C = lambda n: Ctr(nc, n)
    s_const = C("s_const")
    s_cg = C("s_cg")
    s_x = C("s_x")
    s_tab = [[C(f"s_tab{p}{w}") for w in range(2)] for p in range(2)]
    s_wq = C("s_wq")
    s_w2 = [C("s_w2a"), C("s_w2b"), C("s_w2c"), C("s_w2d")]
    s_xn2 = [C("s_xn2a"), C("s_xn2b")]
    s_kvq = [C("s_kvqa"), C("s_kvqb")]
    s_res = [C("s_resa"), C("s_resb")]
    s_w4 = C("s_w4")
    s_fin = C("s_fin")
    st_k = [C("st_ka"), C("st_kb")]
    st_q = [C("st_qa"), C("st_qb")]
    st_v = [C("st_va"), C("st_vb")]
    st_xn = [C("st_xna"), C("st_xnb")]
    st_res = [C("st_resa"), C("st_resb")]
    st_out = [C("st_outa"), C("st_outb")]
    c_t = C("c_t")
    c_v = C("c_v")
    c_s = C("c_s")
    c_qk = [C("c_qkE"), C("c_qkO")]
    c_ex = [C("c_exE"), C("c_exO")]
    c_pv = [C("c_pvE"), C("c_pvO")]
    c_fin = C("c_fin")
    c_e = C("c_e")
    c_fbb = C("c_fbb")
    c_ftt = C("c_ftt")

    r = Rec()
    r.op("sync", "dma_start", out=g_ident[:], in_=ident_bf, inc=s_const.bump(16))
    r.op("sync", "dma_start", out=g_pmat[:], in_=pmat_bf, inc=s_const.bump(16))
    r.op("sync", "dma_start", out=g_ones_bf[:], in_=ones_bf, inc=s_const.bump(16))
    r.op("sync", "dma_start", out=g_ones_f[:], in_=ones_f32, inc=s_const.bump(16))
    r.op("sync", "dma_start", out=g_c64[:], in_=c64_bf, inc=s_const.bump(16))
    r.op("sync", "dma_start", out=g_lnw[:], in_=lnw_d, inc=s_const.bump(16))
    r.op("sync", "dma_start", out=g_lnb[:], in_=lnb_d, inc=s_const.bump(16))
    r.op("gpsimd", "dma_start", out=g_brow[:], in_=bout_d, inc=s_cg.bump(16))
    r.emit(nc)

    es1 = ExitStack()

    def sb1(name, shape, dt):
        return es1.enter_context(nc.sbuf_tensor(name, shape, dt))

    ctab = [[sb1(f"ctab{p}{w}", [128, BLK], F32) for w in range(2)] for p in range(2)]
    lo_sml = [sb1(f"lo_sml{p}", [128, BLK], F32) for p in range(2)]
    wqkv_bf = sb1("wqkv_bf", [128, 8 * 3 * D], BF16)
    x_t = [sb1(f"x_t{i}", [128, D], F32) for i in range(4)]
    xn_tm = [sb1(f"xn_tm{i}", [128, D], BF16) for i in range(4)]
    xn_fm = [sb1(f"xn_fm{i}", [128, 8 * BLK], BF16) for i in range(2)]
    k_blk = [sb1(f"k_blk{i}", [128, NP * BLK], BF16) for i in range(2)]
    q_blk = [sb1(f"q_blk{i}", [128, NP * BLK], BF16) for i in range(2)]
    v_blk = [sb1(f"v_blk{i}", [128, NP * 4 * 256], BF16) for i in range(2)]
    kbf = [sb1(f"kbf{i}", [128, BLK], BF16) for i in range(2)]
    t1_scr = sb1("t1_scr", [128, BLK], F32)
    t2_scr = sb1("t2_scr", [128, BLK], F32)
    sq_scr = [sb1(f"sq_scr{i}", [128, D], BF16) for i in range(4)]
    st_a = sb1("st_a", [128, 4], F32)
    st_b = sb1("st_b", [128, 4], F32)
    st_c = sb1("st_c", [128, 4], F32)
    st_r = sb1("st_r", [128, 4], F32)
    st_nb = sb1("st_nb", [128, 4], F32)

    pK = [es1.enter_context(nc.psum_tensor(f"pK{i}", [128, BLK], F32)) for i in range(2)]
    pR = [es1.enter_context(nc.psum_tensor(f"pR{i}", [128, BLK], F32)) for i in range(2)]
    pV = [es1.enter_context(nc.psum_tensor(f"pV{i}", [128, BLK], F32)) for i in range(2)]
    pT1 = [es1.enter_context(nc.psum_tensor(f"pT1{i}", [128, BLK], BF16)) for i in range(2)]

    r = Rec()
    tab_free = [0, 0]
    tab_ready = [0, 0]

    def load_tables(blk):
        par = blk % 2
        tok0 = blk * BLK
        for w, hi_d, lo_d in ((0, cos_hi_d, cos_lo_d), (1, sin_hi_d, sin_lo_d)):
            r.wait("sync", c_v, tab_free[par] if w == 0 else c_v.v)
            r.op("sync", "dma_start", out=ctab[par][w][:],
                 in_=hi_d[:, tok0:tok0 + BLK], inc=s_tab[par][w].bump(16))
            r.op("sync", "dma_start", out=lo_sml[par][:],
                 in_=lo_d[:, tok0:tok0 + BLK], inc=s_tab[par][w].bump(16))
            r.wait("vector", s_tab[par][w], s_tab[par][w].v)
            r.op("vector", "tensor_add", ctab[par][w][:], ctab[par][w][:],
                 lo_sml[par][:], inc=c_v.bump())
        tab_ready[par] = c_v.v

    load_tables(0)
    r.op("gpsimd", "dma_start", out=wqkv_bf[:],
         in_=wqkv_d.rearrange("(c p) n -> p c n", p=128), inc=s_wq.bump(16))
    r.wait("tensor", s_const)
    r.wait("vector", s_const)
    r.wait("scalar", s_const)

    wfc_rows_g = wfc_d.rearrange("(c p) d -> c p d", p=128)
    wpj_rows_g = wproj_d.rearrange("(c p) (u w) -> c u p w", p=128, w=1024)
    gamma_jobs = [wfc_rows_g[i] for i in range(32)] + \
                 [wpj_rows_g[ci, ui] for ci in range(8) for ui in range(4)]
    gamma_ct = [0]
    gamma_marks = {}

    def emit_gamma(njobs):
        for _ in range(njobs):
            ch = gamma_ct[0]
            if ch >= len(gamma_jobs):
                return
            bi = ch % 4
            if ch - 4 in gamma_marks:
                r.wait("sync", c_v, gamma_marks[ch - 4])
            r.op("sync", "dma_start", out=wc_buf[bi][:], in_=gamma_jobs[ch],
                 inc=s_w2[bi].bump(16))
            r.wait("vector", s_w2[bi], s_w2[bi].v)
            r.op("vector", "tensor_reduce",
                 red96[:, ch * 4:(ch + 1) * 4].rearrange("p (s u) -> p s u", u=1),
                 wc_buf[bi][:].rearrange("p (s u) -> p s u", u=256),
                 AX, ALU.add, apply_absolute_value=True, inc=c_v.bump())
            gamma_marks[ch] = c_v.v
            gamma_ct[0] += 1

    pk_last, pr_last, kbf_last = {}, {}, {}

    cv_xnfm_mark = [0, 0]

    def emit_lnstats(blk):
        nonlocal_vars = None
        tok0 = blk * BLK
        sxb = s_x.v
        r.wait("sync", c_s, ln_state["mark_xload_s"])
        r.wait("sync", c_v, ln_state["mark_xload_v"])
        for tt in range(4):
            r.op("sync", "dma_start", out=x_t[tt][:],
                 in_=x_d[tok0 + tt * 128: tok0 + (tt + 1) * 128, :], inc=s_x.bump(16))
        if blk >= 1:
            load_tables(blk)
        emit_gamma(8)
        r.wait("scalar", c_v, ln_state["prev_cv_stats"])
        for tt in range(4):
            r.wait("vector", s_x, sxb + 64)
            r.op("vector", "reduce_sum", st_a[:, tt:tt + 1], x_t[tt][:], AX,
                 inc=(c_e.bump() if tt == 3 else c_v.bump()))
            if tt == 3:
                r.wait("vector", c_e)
                r.op("vector", "engine_nop", inc=c_v.bump())
            r.wait("scalar", s_x, sxb + 64)
            r.op("scalar", "activation", sq_scr[tt][:], x_t[tt][:], AF.Square,
                 accum_out=st_b[:, tt:tt + 1], inc=c_s.bump())
        ln_state["mark_xload_v"] = c_v.v
        r.wait("vector", c_s)

        def vedge():
            r.q["vector"][-1] = r.q["vector"][-1][:4] + (c_e.bump(),)
            r.wait("vector", c_e)

        r.op("vector", "tensor_scalar_mul", st_a[:], st_a[:], 1.0 / D)
        vedge()
        r.op("vector", "tensor_scalar_mul", st_b[:], st_b[:], 1.0 / D)
        vedge()
        r.op("vector", "tensor_mul", st_c[:], st_a[:], st_a[:])
        vedge()
        r.op("vector", "tensor_sub", st_b[:], st_b[:], st_c[:])
        vedge()
        r.op("vector", "tensor_scalar_add", st_b[:], st_b[:], EPS_LN, inc=c_v.bump())
        r.wait("scalar", c_v)
        r.op("scalar", "activation", st_c[:], st_b[:], AF.Sqrt, inc=c_s.bump())
        r.wait("vector", c_s)
        r.op("vector", "reciprocal", st_r[:], st_c[:])
        vedge()
        r.op("vector", "tensor_mul", st_c[:], st_r[:], st_r[:])
        vedge()
        r.op("vector", "tensor_mul", st_c[:], st_c[:], st_b[:])
        vedge()
        r.op("vector", "tensor_scalar", st_c[:], st_c[:], -0.5, 1.5, ALU.mult, ALU.add)
        vedge()
        r.op("vector", "tensor_mul", st_r[:], st_r[:], st_c[:])
        vedge()
        r.op("vector", "tensor_mul", st_nb[:], st_a[:], st_r[:])
        vedge()
        r.op("vector", "tensor_scalar_mul", st_nb[:], st_nb[:], -1.0, inc=c_v.bump())
        cv_stats = c_v.v
        ln_state["prev_cv_stats"] = cv_stats
        r.wait("scalar", c_v, cv_stats)
        r.wait("scalar", c_t, ln_state["mark_tp"])
        for tt in range(4):
            r.op("scalar", "activation", xn_tm[tt][:], x_t[tt][:], AF.Identity,
                 bias=st_nb[:, tt:tt + 1], scale=st_r[:, tt:tt + 1], inc=c_s.bump())
        ln_state["mark_xload_s"] = c_s.v

    def emit_transposes(blk):
        bi = blk % 2
        r.wait("tensor", c_s, ln_state["mark_xload_s"])
        r.wait("tensor", st_xn[bi], st_xn[bi].v)
        r.wait("tensor", c_t, ln_state["mark_xnfm_free"][bi])
        r.wait("tensor", c_v)
        for d in range(8):
            if d >= 2:
                r.wait("tensor", c_v)
            for tt in range(4):
                r.op("tensor", "transpose", pT1[d % 2][:, tt * 128:(tt + 1) * 128],
                     xn_tm[tt][:, d * 128:(d + 1) * 128], g_ident[:],
                     inc=(c_t.bump() if tt == 3 else None))
            r.wait("vector", c_t)
            r.op("vector", "tensor_copy", xn_fm[bi][:, d * BLK:(d + 1) * BLK],
                 pT1[d % 2][:], inc=c_v.bump())
        ln_state["mark_tp"] = c_t.v
        cv_xnfm_mark[bi] = c_v.v
        if blk < 4:
            tok0 = blk * BLK
            r.wait("gpsimd", c_v, cv_xnfm_mark[bi])
            r.op("gpsimd", "dma_start", out=xn_s[:, :, tok0:tok0 + BLK],
                 in_=xn_fm[bi][:].rearrange("p (d t) -> p d t", t=BLK),
                 inc=st_xn[bi].bump(16))

    def emit_qkv(blk):
        own_blk = blk < 4
        tok0 = blk * BLK
        bi = blk % 2
        r.wait("tensor", s_wq, 16)
        r.wait("tensor", c_v, cv_xnfm_mark[bi])

        def proj_rope(col0, buf, jbase):
            ct_proj = {}
            cv_kbf = {}
            for p in range(NP + 1):
                if p < NP:
                    j = jbase + p
                    pk = pK[j % 2]
                    if j % 2 in pk_last:
                        r.wait("tensor", c_v, pk_last[j % 2])
                    for d in range(8):
                        r.op("tensor", "matmul", pk[:],
                             wqkv_bf[:, d * 3 * D + col0 + p * 128: d * 3 * D + col0 + (p + 1) * 128],
                             xn_fm[bi][:, d * BLK:(d + 1) * BLK],
                             start=(d == 0), stop=(d == 7),
                             inc=(c_t.bump() if d == 7 else None))
                    ct_proj[p] = c_t.v
                    if j % 2 in kbf_last:
                        r.wait("vector", c_t, kbf_last[j % 2])
                    r.wait("vector", c_t, ct_proj[p])
                    r.op("vector", "tensor_copy", kbf[j % 2][:], pk[:], inc=c_v.bump())
                    cv_kbf[p] = c_v.v
                if p >= 1:
                    pm1 = p - 1
                    jm = jbase + pm1
                    pkm, prm, kbm = pK[jm % 2], pR[jm % 2], kbf[jm % 2]
                    r.wait("tensor", c_v, cv_kbf[pm1])
                    if jm % 2 in pr_last:
                        r.wait("tensor", c_v, pr_last[jm % 2])
                    r.op("tensor", "matmul", prm[:], g_pmat[:], kbm[:],
                         start=True, stop=True, inc=c_t.bump())
                    kbf_last[jm % 2] = c_t.v
                    ct_rot = c_t.v
                    r.op("vector", "tensor_mul", t1_scr[:], pkm[:],
                         ctab[blk % 2][0][:], inc=c_v.bump())
                    pk_last[jm % 2] = c_v.v
                    r.wait("vector", c_t, ct_rot)
                    r.op("vector", "tensor_mul", t2_scr[:], prm[:],
                         ctab[blk % 2][1][:], inc=c_v.bump())
                    pr_last[jm % 2] = c_v.v
                    r.op("vector", "tensor_add", buf[:, pm1 * BLK:(pm1 + 1) * BLK],
                         t1_scr[:], t2_scr[:], inc=c_v.bump())

        r.wait("vector", st_k[bi], st_k[bi].v)
        proj_rope(D, k_blk[bi], blk * 24)
        r.wait("gpsimd", c_v)
        r.op("gpsimd", "dma_start", out=k_s[:, :, tok0:tok0 + BLK],
             in_=k_blk[bi][:].rearrange("p (n t) -> p n t", t=BLK),
             inc=st_k[bi].bump(16))
        if own_blk:
            r.wait("vector", st_q[bi], st_q[bi].v)
            proj_rope(0, q_blk[bi], blk * 24 + 8)
            r.wait("gpsimd", c_v)
            r.op("gpsimd", "dma_start", out=q_s[:, :, tok0:tok0 + BLK],
                 in_=q_blk[bi][:].rearrange("p (n t) -> p n t", t=BLK),
                 inc=st_q[bi].bump(16))
        vb = v_blk[bi]
        r.wait("vector", st_v[bi], st_v[bi].v)
        if blk < 2:
            r.op("vector", "memset",
                 vb[:].rearrange("p (k c) -> p k c", c=256)[:, :, 64:192], 1.0,
                 inc=c_v.bump())
        r.wait("tensor", c_v)
        for tt in range(4):
            for cc in range(2):
                jv = tt * 2 + cc
                pv = pV[jv % 2]
                if jv >= 2:
                    r.wait("tensor", c_v)
                for d in range(8):
                    r.op("tensor", "matmul", pv[:],
                         xn_fm[bi][:, d * BLK + tt * 128: d * BLK + (tt + 1) * 128],
                         wqkv_bf[:, d * 3 * D + 2 * D + cc * 512: d * 3 * D + 2 * D + (cc + 1) * 512],
                         start=(d == 0), stop=(d == 7),
                         inc=(c_t.bump() if d == 7 else None))
                r.wait("vector", c_t)
                srcv = pv[:].rearrange("p (n j c) -> p n j c", j=2, c=64)
                vbv = vb[:].rearrange("p (n k c) -> p n k c", k=4, c=256)
                r.op("vector", "tensor_copy",
                     vbv[:, cc * 4:(cc + 1) * 4, tt, 0:64], srcv[:, :, 0, :])
                r.op("vector", "tensor_copy",
                     vbv[:, cc * 4:(cc + 1) * 4, tt, 192:256], srcv[:, :, 1, :],
                     inc=c_v.bump())
        ln_state["mark_xnfm_free"][bi] = c_t.v
        tab_free[blk % 2] = c_v.v
        r.wait("gpsimd", c_v)
        r.op("gpsimd", "dma_start",
             out=v_s[:, :, blk * 4 * 256:(blk + 1) * 4 * 256],
             in_=vb[:].rearrange("p (n c) -> p n c", c=4 * 256),
             inc=st_v[bi].bump(16))

    ln_state = {"mark_xload_s": 0, "mark_xload_v": c_v.v, "prev_cv_stats": 0,
                "mark_xnfm_free": [0, 0], "mark_tp": 0}
    emit_lnstats(0)
    emit_transposes(0)
    for blk in range(NBLK):
        if blk + 1 < NBLK:
            emit_lnstats(blk + 1)
        emit_qkv(blk)
        if blk + 1 < NBLK:
            emit_transposes(blk + 1)
    emit_gamma(len(gamma_jobs))

    def finish_thr(cols, thr):
        r.op("vector", "tensor_reduce", acc1[:], red96[:, cols[0] * 4:cols[1] * 4],
             AX, ALU.add, inc=c_e.bump())
        r.wait("vector", c_e)
        r.op("vector", "engine_nop", inc=c_v.bump())
        r.wait("tensor", c_v)
        for par in (0, 1):
            if par in pr_last:
                r.wait("tensor", c_v, pr_last[par])
        r.op("tensor", "matmul", pR[0][0:1, 0:1], g_ones_f[:, 0:1], acc1[:],
             start=True, stop=True, inc=c_t.bump())
        r.wait("vector", c_t)
        r.op("vector", "tensor_scalar", thr11[:], pR[0][0:1, 0:1], 0.5 / (DFF * D),
             0.5e-6, ALU.mult, ALU.add, inc=c_v.bump())
        r.wait("tensor", c_v)
        r.op("tensor", "matmul", pR[0][:, 0:1], g_ones_f[0:1, 0:128], thr11[:],
             start=True, stop=True, inc=c_t.bump())
        r.wait("vector", c_t)
        r.op("vector", "tensor_copy", thr[:, 0:1], pR[0][:, 0:1])
        r.op("vector", "tensor_scalar_mul", thr[:, 1:2], pR[0][:, 0:1], -1.0,
             inc=c_v.bump())

    finish_thr((0, 32), thr_fc)
    finish_thr((32, 64), thr_pj)
    r.op("vector", "engine_nop", inc=c_e.bump())
    r.wait("vector", c_e)
    r.op("vector", "engine_nop", inc=c_v.bump())
    cv_thr = c_v.v

    p1_end = dict(c_v=c_v.v, c_s=c_s.v, c_t=c_t.v)
    p1_stores = [(st_k[0], st_k[0].v), (st_k[1], st_k[1].v),
                 (st_q[0], st_q[0].v), (st_q[1], st_q[1].v),
                 (st_v[0], st_v[0].v), (st_v[1], st_v[1].v),
                 (st_xn[0], st_xn[0].v), (st_xn[1], st_xn[1].v)]
    r.emit(nc)
    es1.close()

    es2 = ExitStack()

    def sb2(name, shape, dt):
        return es2.enter_context(nc.sbuf_tensor(name, shape, dt))

    wqfcT = sb2("wqfcT", [128, 8 * DFF], BF16)
    wqprojT = sb2("wqprojT", [128, 32 * D], BF16)
    h_sb = sb2("h_sb", [128, 32 * TT2], BF16)
    xn_t = [sb2(f"xn_t{i}", [128, 8 * TT2], BF16) for i in range(2)]
    mu_sb = sb2("mu_sb", [128, TT2], F32)
    ex2_sb = sb2("ex2_sb", [128, TT2], F32)
    var_sb = sb2("var_sb", [128, TT2], F32)
    rr_sb = sb2("rr_sb", [128, TT2], F32)
    tln_scr = sb2("tln_scr", [128, TT2], F32)
    x2_t = [sb2(f"x2_t{i}", [128, D], F32) for i in range(2)]
    out2_t = [sb2(f"out2_t{i}", [128, D], F32) for i in range(2)]

    es2a = ExitStack()
    pT2 = [es2a.enter_context(nc.psum_tensor(f"pT2{i}", [128, 512], BF16)) for i in range(2)]

    r = Rec()
    for e in ENGINES:
        for ctr, vv in p1_stores:
            r.wait(e, ctr, vv)
        r.wait(e, c_v, p1_end["c_v"])
        r.wait(e, c_s, p1_end["c_s"])
        r.wait(e, c_t, p1_end["c_t"])

    wfc_rows = wfc_d.rearrange("(c p) d -> c p d", p=128)
    wpj_rows = wproj_d.rearrange("(c p) n -> c p n", p=128)

    tq_marks = {}
    tqt_marks = {}

    def quant_transpose(load_ap, thr, ntiles, dst_fn, qi):
        bi = qi % 2
        w = load_ap.shape[-1]
        if qi - 2 in tq_marks:
            r.wait("sync", c_v, tq_marks[qi - 2])
        else:
            r.wait("sync", c_v, cv_thr)
        r.op("sync", "dma_start", out=wc_buf[bi][:, 0:w], in_=load_ap,
             inc=s_w2[bi].bump(16))
        r.wait("vector", s_w2[bi], s_w2[bi].v)
        r.wait("vector", c_v, cv_thr)
        if qi - 2 in tqt_marks:
            r.wait("vector", c_t, tqt_marks[qi - 2])
        wv = wc_buf[bi][:, 0:w]
        qn = wqn_buf[bi][:, 0:w]
        r.op("vector", "tensor_single_scalar", qn, wv, thr[:, 1:2], ALU.is_lt)
        r.op("vector", "scalar_tensor_tensor", qn, wv, thr[:, 0:1], qn,
             ALU.is_gt, ALU.subtract, inc=c_v.bump())
        tq_marks[qi] = c_v.v
        cv_q = c_v.v
        evac_marks = {}
        for grp in range(ntiles // 4):
            pt = pT2[grp % 2]
            r.wait("tensor", c_v, evac_marks.get(grp - 2, cv_q))
            for u in range(4):
                ti = grp * 4 + u
                r.op("tensor", "transpose", pt[:, u * 128:(u + 1) * 128],
                     qn[:, ti * 128:(ti + 1) * 128], g_ident[:],
                     inc=(c_t.bump() if u == 3 else None))
            r.wait("vector", c_t)
            r.op("vector", "tensor_copy", dst_fn(grp), pt[:].rearrange(
                "p (u c) -> p u c", c=128), inc=c_v.bump())
            evac_marks[grp] = c_v.v
        tqt_marks[qi] = c_t.v

    qi = 0
    wqfcT_v = wqfcT[:].rearrange("p (d h) -> p d h", h=DFF)
    for c in range(32):
        def dst_fc(grp, c=c):
            return wqfcT_v[:, 4 * grp:4 * grp + 4, c * 128:(c + 1) * 128]
        quant_transpose(wfc_rows[c], thr_fc, 8, dst_fc, qi)
        qi += 1
    wqprojT_v = wqprojT[:].rearrange("p (m d) -> p m d", d=D)
    for ci in range(8):
        for ui in range(4):
            def dst_pj(grp, ci=ci, ui=ui):
                return wqprojT_v[:, 8 * ui + 4 * grp: 8 * ui + 4 * grp + 4,
                                 ci * 128:(ci + 1) * 128]
            quant_transpose(wpj_rows[ci][:, ui * 1024:(ui + 1) * 1024], thr_pj, 8,
                            dst_pj, qi)
            qi += 1

    cv_wq_done = c_v.v
    if globals().get("_DBG_WQ"):
        _k = "ExternalOutput" if globals().get("_DBG_IO") else "Internal"
        dbg_fc = nc.dram_tensor("dbg_fc", [128, 8 * DFF], BF16, kind=_k).ap()
        dbg_pj = nc.dram_tensor("dbg_pj", [128, 32 * D], BF16, kind=_k).ap()
        r.wait("gpsimd", c_v, cv_wq_done)
        r.op("gpsimd", "dma_start", out=dbg_fc, in_=wqfcT[:], inc=st_k[0].bump(16))
        r.op("gpsimd", "dma_start", out=dbg_pj, in_=wqprojT[:], inc=st_k[0].bump(16))
    es2a.close()
    h2_ring = sb2("h2_ring", [128, 4 * TT2], BF16)
    pF = [es2.enter_context(nc.psum_tensor(f"pF{i}", [128, TT2], F32)) for i in range(2)]
    pS1 = es2.enter_context(nc.psum_tensor("pS1", [128, TT2], F32))
    pS2 = es2.enter_context(nc.psum_tensor("pS2", [128, TT2], F32))
    pP2 = [es2.enter_context(nc.psum_tensor(f"pP2{i}", [128, D], F32)) for i in range(2)]

    xn_marks = {}
    res_marks = {}
    for ti in range(OWN // TT2):
        bi = ti % 2
        if ti - 2 in xn_marks:
            r.wait("sync", c_t, xn_marks[ti - 2])
        r.op("sync", "dma_start", out=xn_t[bi][:],
             in_=xn_s[:, :, ti * TT2:(ti + 1) * TT2], inc=s_xn2[bi].bump(16))
        r.wait("tensor", s_xn2[bi], s_xn2[bi].v)
        r.wait("tensor", c_v, cv_wq_done)
        gelu_marks = {}
        h2_done = None
        for m in range(32):
            pf = pF[m % 2]
            if m - 2 in gelu_marks:
                r.wait("tensor", c_s, gelu_marks[m - 2])
            for d in range(8):
                r.op("tensor", "matmul", pf[:],
                     wqfcT[:, d * DFF + m * 128: d * DFF + (m + 1) * 128],
                     xn_t[bi][:, d * TT2:(d + 1) * TT2],
                     start=(d == 0), stop=(d == 7),
                     inc=(c_t.bump() if d == 7 else None))
            r.wait("scalar", c_t)
            r.op("scalar", "activation", h_sb[:, m * TT2:(m + 1) * TT2], pf[:],
                 AF.Gelu, inc=c_s.bump())
            gelu_marks[m] = c_s.v
        cs_gelu_all = c_s.v
        r.wait("tensor", c_s, cs_gelu_all)
        for m in range(32):
            r.op("tensor", "matmul", pS1[:], g_ones_bf[:],
                 h_sb[:, m * TT2:(m + 1) * TT2], start=(m == 0), stop=(m == 31),
                 skip_group_check=True, inc=(c_t.bump() if m == 31 else None))
        h2_marks = {}
        ps2_marks = {}
        for m in range(32):
            sl = m % 4
            r.wait("vector", c_s, gelu_marks[m])
            if m - 4 in ps2_marks:
                r.wait("vector", c_t, ps2_marks[m - 4])
            r.op("vector", "tensor_mul",
                 h2_ring[:, sl * TT2:(sl + 1) * TT2], h_sb[:, m * TT2:(m + 1) * TT2],
                 h_sb[:, m * TT2:(m + 1) * TT2], inc=c_v.bump())
            h2_marks[m] = c_v.v
            r.wait("tensor", c_v, h2_marks[m])
            r.op("tensor", "matmul", pS2[:], g_ones_bf[:],
                 h2_ring[:, sl * TT2:(sl + 1) * TT2], start=(m == 0), stop=(m == 31),
                 skip_group_check=True, inc=c_t.bump())
            ps2_marks[m] = c_t.v
        xn_marks[ti] = c_t.v
        ct_stats = c_t.v
        r.wait("vector", c_t, ct_stats)
        r.wait("vector", c_s)
        r.op("vector", "tensor_scalar_mul", mu_sb[:], pS1[:], 1.0 / DFF)
        r.op("vector", "tensor_scalar_mul", ex2_sb[:], pS2[:], 1.0 / DFF)
        r.op("vector", "tensor_mul", var_sb[:], mu_sb[:], mu_sb[:])
        r.op("vector", "tensor_sub", var_sb[:], ex2_sb[:], var_sb[:])
        r.op("vector", "tensor_scalar_add", var_sb[:], var_sb[:], EPS_LN, inc=c_v.bump())
        r.wait("scalar", c_v)
        r.op("scalar", "activation", ex2_sb[:], var_sb[:], AF.Sqrt, inc=c_s.bump())
        r.wait("vector", c_s)
        r.op("vector", "reciprocal", rr_sb[:], ex2_sb[:])
        r.op("vector", "tensor_mul", tln_scr[:], rr_sb[:], rr_sb[:])
        r.op("vector", "tensor_mul", tln_scr[:], tln_scr[:], var_sb[:])
        r.op("vector", "tensor_scalar", tln_scr[:], tln_scr[:], -0.5, 1.5, ALU.mult, ALU.add)
        r.op("vector", "tensor_mul", rr_sb[:], rr_sb[:], tln_scr[:], inc=c_v.bump())
        cv_r_done = c_v.v
        for tm in range(TT2 // 128):
            sub = ti * (TT2 // 128) + tm
            bi2 = tm
            if (ti - 1, tm) in res_marks:
                r.wait("sync", c_v, res_marks[(ti - 1, tm)])
            row0 = ti * TT2 + tm * 128
            r.op("sync", "dma_start", out=x2_t[bi2][:], in_=x_d[row0:row0 + 128, :],
                 inc=s_res[bi2].bump(16))
        for m in range(32):
            r.op("vector", "tensor_sub", tln_scr[:], h_sb[:, m * TT2:(m + 1) * TT2], mu_sb[:])
            r.op("vector", "tensor_mul", tln_scr[:], tln_scr[:], rr_sb[:])
            r.op("vector", "tensor_scalar", h_sb[:, m * TT2:(m + 1) * TT2], tln_scr[:],
                 g_lnw[:, m:m + 1], g_lnb[:, m:m + 1], ALU.mult, ALU.add,
                 inc=c_v.bump())
            cv_norm_m = c_v.v
            if m == 0:
                for tm in range(TT2 // 128):
                    if (ti - 1, tm) in res_marks:
                        r.wait("tensor", c_v, res_marks[(ti - 1, tm)])
            r.wait("tensor", c_v, cv_norm_m)
            for tm in range(TT2 // 128):
                for n in range(2):
                    r.op("tensor", "matmul", pP2[tm][:, n * 512:(n + 1) * 512],
                         h_sb[:, m * TT2 + tm * 128: m * TT2 + (tm + 1) * 128],
                         wqprojT[:, m * D + n * 512: m * D + (n + 1) * 512],
                         start=(m == 0), stop=(m == 31), skip_group_check=True,
                         inc=(c_t.bump() if (m == 31 and n == 1 and tm == TT2 // 128 - 1)
                              else None))
        ct_proj = c_t.v
        for tm in range(TT2 // 128):
            sub = ti * (TT2 // 128) + tm
            bi2 = tm
            row0 = ti * TT2 + tm * 128
            r.wait("vector", c_t, ct_proj)
            r.wait("vector", s_res[bi2], s_res[bi2].v)
            r.wait("vector", st_res[bi2], st_res[bi2].v)
            r.op("vector", "tensor_add", out2_t[bi2][:], pP2[tm][:], x2_t[bi2][:],
                 inc=c_v.bump())
            res_marks[(ti, tm)] = c_v.v
            r.wait("gpsimd", c_v, res_marks[(ti, tm)])
            r.op("gpsimd", "dma_start", out=out_d[row0:row0 + 128, :],
                 in_=out2_t[bi2][:], inc=st_res[bi2].bump(16))

    p2_end = dict(c_v=c_v.v, c_s=c_s.v, c_t=c_t.v)
    p2_stores = [(st_res[0], st_res[0].v), (st_res[1], st_res[1].v)]
    r.emit(nc)
    es2.close()

    es34 = ExitStack()
    attn_all = [es34.enter_context(nc.sbuf_tensor(f"attn_all{i}", [128, OWN], BF16))
                for i in range(NP)]
    es3 = ExitStack()

    def sb3(name, shape, dt):
        return es3.enter_context(nc.sbuf_tensor(name, shape, dt))

    kb = [sb3(f"kb{i}", [128, T], BF16) for i in range(2)]
    qb = [sb3(f"qb{i}", [128, OWN], BF16) for i in range(2)]
    kb2 = [sb3(f"kb2{i}", [128, T], BF16) for i in range(2)]
    qb2 = [sb3(f"qb2{i}", [128, OWN], BF16) for i in range(2)]
    vb3 = [sb3(f"vb3{i}", [128, 32 * 256], BF16) for i in range(2)]
    probs = [[sb3(f"probs{s}{i}", [128, 1024], BF16) for i in range(4)] for s in range(1)]
    recip_sb = sb3("recip_sb", [128, 1024], BF16)
    bb_sb = sb3("bb_sb", [128, 1024], F32)
    pvsum_sb = sb3("pvsum_sb", [128, 1024], F32)

    pQK = [es3.enter_context(nc.psum_tensor(f"pQK{i}", [128, 1024], F32)) for i in range(2)]
    pPVa = es3.enter_context(nc.psum_tensor("pPVa", [128, 1024], F32))
    pPVb = es3.enter_context(nc.psum_tensor("pPVb", [128, 1024], F32))

    r = Rec()
    for e in ENGINES:
        for ctr, vv in p2_stores:
            r.wait(e, ctr, vv)
        r.wait(e, c_v, p2_end["c_v"])
        r.wait(e, c_s, p2_end["c_s"])
        r.wait(e, c_t, p2_end["c_t"])

    fin_mark = {}
    kvq_mark = {}
    fin_count = 0
    for pi in range(NP):
        bib = pi % 2
        if pi - 2 in fin_mark:
            r.wait("sync", c_pv[0], fin_mark[pi - 2])
        r.op("sync", "dma_start", out=kb[bib][:], in_=k_s[:, pi, :],
             inc=s_kvq[bib].bump(16))
        r.op("sync", "dma_start", out=qb[bib][:], in_=q_s[:, pi, :],
             inc=s_kvq[bib].bump(16))
        r.op("sync", "dma_start", out=vb3[bib][:], in_=v_s[:, pi, :],
             inc=s_kvq[bib].bump(16))
        r.op("sync", "dma_start", out=kb2[bib][64:128, :], in_=k_s[0:64, pi, :],
             inc=s_kvq[bib].bump(16))
        r.op("sync", "dma_start", out=kb2[bib][0:64, :], in_=k_s[64:128, pi, :],
             inc=s_kvq[bib].bump(16))
        r.op("sync", "dma_start", out=qb2[bib][64:128, :], in_=q_s[0:64, pi, :],
             inc=s_kvq[bib].bump(16))
        r.op("sync", "dma_start", out=qb2[bib][0:64, :], in_=q_s[64:128, pi, :],
             inc=s_kvq[bib].bump(16))
        kvq_mark[pi] = s_kvq[bib].v
        for qc in range(2):
            for side in range(2):
                r0 = 64 * side
                d0, d1 = (64, 128) if side == 0 else (0, 64)
                v0, v1 = (0, 64) if side == 0 else (64, 128)
                e0, q0, p0 = c_ex[0].v, c_qk[0].v, c_pv[0].v

                def emit_pv(ktp, bib=bib, side=side, e0=e0, p0=p0):
                    if ktp == 0:
                        r.wait("tensor", c_ftt)
                    r.wait("tensor", c_ex[0], e0 + ktp + 1)
                    kc = ktp * 256 + side * 128
                    for n in range(2):
                        r.op("tensor", "matmul", pPVa[:, n * 512:(n + 1) * 512],
                             vb3[bib][0:64, kc:kc + 128],
                             probs[0][ktp % 4][0:64, n * 512:(n + 1) * 512],
                             start=(ktp == 0), stop=(ktp == 31), skip_group_check=True,
                             tile_position=(0, 0))
                        r.op("tensor", "matmul", pPVb[:, n * 512:(n + 1) * 512],
                             vb3[bib][64:128, kc:kc + 128],
                             probs[0][ktp % 4][64:128, n * 512:(n + 1) * 512],
                             start=(ktp == 0), stop=(ktp == 31), skip_group_check=True,
                             tile_position=(64, 0),
                             inc=(c_pv[0].bump() if n == 1 else None))

                for s in range(16):
                    kt0, kt1 = 2 * s, 2 * s + 1
                    if s == 0:
                        r.wait("tensor", s_kvq[bib], kvq_mark[pi])
                    r.wait("tensor", c_ex[0], e0 + kt0)
                    for kt in (kt0, kt1):
                        pq = pQK[kt % 2]
                        if kt % 2 == 0:
                            ksrc, qsrc, rr0 = kb[bib], qb[bib], r0
                        else:
                            ksrc, qsrc, rr0 = kb2[bib], qb2[bib], 64 - r0
                        for n in range(2):
                            r.op("tensor", "matmul", pq[:, n * 512:(n + 1) * 512],
                                 ksrc[rr0:rr0 + 64, kt * 128:(kt + 1) * 128],
                                 qsrc[rr0:rr0 + 64, qc * 1024 + n * 512: qc * 1024 + (n + 1) * 512],
                                 start=True, stop=True, tile_position=(rr0, 0),
                                 inc=(c_qk[0].bump() if n == 1 else None))
                    for kt in (kt0, kt1):
                        r.wait("scalar", c_qk[0], q0 + kt + 1)
                        r.wait("scalar", c_pv[0], p0 + kt - 3)
                        r.op("scalar", "activation", probs[0][kt % 4][:], pQK[kt % 2][:],
                             AF.Exp, scale=0.125, inc=c_ex[0].bump())
                    if s >= 1:
                        emit_pv(kt0 - 2)
                        emit_pv(kt1 - 2)
                emit_pv(30)
                emit_pv(31)
                r.wait("vector", c_pv[0])
                r.op("vector", "tensor_copy", pvsum_sb[:], pPVa[:])
                r.op("vector", "tensor_add", pvsum_sb[:], pvsum_sb[:], pPVb[:],
                     inc=c_ftt.bump())
                r.op("vector", "reciprocal", recip_sb[d0:d1, :], pvsum_sb[d0:d1, :],
                     inc=c_v.bump())
                r.wait("sync", c_v)
                r.op("sync", "dma_start", out=recip_sb[v0:v1, :],
                     in_=recip_sb[d0:d1, :], inc=s_fin.bump(16))
                r.wait("vector", s_fin, s_fin.v)
                r.op("vector", "tensor_mul",
                     attn_all[pi][v0:v1, qc * 1024:(qc + 1) * 1024],
                     pvsum_sb[v0:v1, :], recip_sb[v0:v1, :], inc=c_v.bump())
                fin_count += 1
        fin_mark[pi] = c_pv[0].v

    p3_end = dict(c_fin=c_fin.v, c_v=c_v.v, c_t=c_t.v)
    r.emit(nc)
    es3.close()

    es4 = ExitStack()

    def sb4(name, shape, dt):
        return es4.enter_context(nc.sbuf_tensor(name, shape, dt))

    wout_bf = sb4("wout_bf", [128, 8 * D], BF16)
    res_t = [sb4(f"res_t{i}", [128, D], F32) for i in range(2)]
    out4_t = [sb4(f"out4_t{i}", [128, D], F32) for i in range(2)]
    pP4 = [es4.enter_context(nc.psum_tensor(f"pP4{i}", [128, D], F32)) for i in range(2)]

    r = Rec()
    for e in ENGINES:
        r.wait(e, c_fin, p3_end["c_fin"])
        r.wait(e, c_v, p3_end["c_v"])
        r.wait(e, c_t, p3_end["c_t"])
    r.op("gpsimd", "dma_start", out=wout_bf[:],
         in_=wout_d.rearrange("(r p) n -> p r n", p=128), inc=s_w4.bump(16))
    out_marks = {}
    for ctr, vv in p2_stores:
        r.wait("sync", ctr, vv)
    for tm in range(OWN // 128):
        bi4 = tm % 2
        if tm - 2 in out_marks:
            r.wait("sync", c_v, out_marks[tm - 2])
        r.op("sync", "dma_start", out=res_t[bi4][:],
             in_=out_d[tm * 128:(tm + 1) * 128, :], inc=s_res[bi4].bump(16))
        r.wait("tensor", s_w4, 16)
        r.wait("tensor", s_const)
        r.wait("tensor", s_cg, 16)
        if tm - 2 in out_marks:
            r.wait("tensor", c_v, out_marks[tm - 2])
        for n in range(2):
            for pi in range(NP):
                r.op("tensor", "matmul", pP4[bi4][:, n * 512:(n + 1) * 512],
                     attn_all[pi][:, tm * 128:(tm + 1) * 128],
                     wout_bf[:, pi * D + n * 512: pi * D + (n + 1) * 512],
                     start=(pi == 0), stop=False)
            r.op("tensor", "matmul", pP4[bi4][:, n * 512:(n + 1) * 512],
                 g_ones_bf[0:1, 0:128], g_brow[0:1, n * 512:(n + 1) * 512],
                 start=False, stop=True,
                 inc=(c_t.bump() if n == 1 else None))
        r.wait("vector", c_t)
        r.wait("vector", s_res[bi4], s_res[bi4].v)
        r.wait("vector", st_out[bi4], st_out[bi4].v)
        r.op("vector", "tensor_add", out4_t[bi4][:], pP4[bi4][:], res_t[bi4][:],
             inc=c_v.bump())
        out_marks[tm] = c_v.v
        r.wait("gpsimd", c_v, out_marks[tm])
        r.op("gpsimd", "dma_start", out=out_d[tm * 128:(tm + 1) * 128, :],
             in_=out4_t[bi4][:], inc=st_out[bi4].bump(16))
    r.wait("gpsimd", st_out[0], st_out[0].v)
    r.wait("gpsimd", st_out[1], st_out[1].v)
    r.emit(nc)
    es4.close()
    es34.close()
    ges.close()
    return nc


_NC_CACHE = [None]


def kernel(L0_attn_x, L0_attn_w_qkv, L0_attn_w_out, L0_attn_b_out,
           L0_ffn_W_fc, L0_ffn_ln_w, L0_ffn_ln_b, L0_ffn_W_proj):
    x = np.ascontiguousarray(np.asarray(L0_attn_x, dtype=np.float32))
    wqkv = np.ascontiguousarray(np.asarray(L0_attn_w_qkv, dtype=np.float32))
    wout = np.ascontiguousarray(np.asarray(L0_attn_w_out, dtype=np.float32))
    bout = np.ascontiguousarray(np.asarray(L0_attn_b_out, dtype=np.float32)).reshape(1, D)
    wfc = np.ascontiguousarray(np.asarray(L0_ffn_W_fc, dtype=np.float32))
    lnw = np.ascontiguousarray(np.asarray(L0_ffn_ln_w, dtype=np.float32).reshape(32, 128).T)
    lnb = np.ascontiguousarray(np.asarray(L0_ffn_ln_b, dtype=np.float32).reshape(32, 128).T)
    wproj = np.ascontiguousarray(np.asarray(L0_ffn_W_proj, dtype=np.float32))

    theta = 1.0 / (10000.0 ** (np.arange(0, DH, 2, dtype=np.float64) / DH))
    ang = np.concatenate([np.outer(np.arange(T, dtype=np.float64), theta)] * 2, axis=1)
    cosT = np.cos(ang).T
    sinT = np.sin(ang).T
    cos128 = np.concatenate([cosT, cosT], axis=0)
    sin128 = np.concatenate([sinT, sinT], axis=0)
    cos_hi = cos128.astype(np.float32)
    cos_lo = (cos128 - cos_hi).astype(np.float32)
    sin_hi = sin128.astype(np.float32)
    sin_lo = (sin128 - sin_hi).astype(np.float32)

    if _NC_CACHE[0] is None:
        _NC_CACHE[0] = build_program()
    nc = _NC_CACHE[0]

    in_maps = []
    for c in range(8):
        b, hh = c // 2, c % 2
        sh = hh * OWN
        in_maps.append({
            "x": np.ascontiguousarray(np.roll(x[b], -sh, axis=0)),
            "wqkv": wqkv, "wout": wout, "bout": bout, "wfc": wfc,
            "lnw": lnw, "lnb": lnb, "wproj": wproj,
            "cos_hi": np.ascontiguousarray(np.roll(cos_hi, -sh, axis=1)),
            "cos_lo": np.ascontiguousarray(np.roll(cos_lo, -sh, axis=1)),
            "sin_hi": np.ascontiguousarray(np.roll(sin_hi, -sh, axis=1)),
            "sin_lo": np.ascontiguousarray(np.roll(sin_lo, -sh, axis=1)),
        })

    res = run_bass_kernel_spmd(nc, in_maps, core_ids=list(range(8)))
    out = np.empty((B, T, D), dtype=np.float32)
    for c in range(8):
        b, hh = c // 2, c % 2
        out[b, hh * OWN:(hh + 1) * OWN, :] = res.results[c]["out"]
    return out


# version 19 (final)
# speedup vs baseline: 1.1637x; 1.1573x over previous
import numpy as np
import ml_dtypes
from contextlib import ExitStack
import concourse.bass as bass
import concourse.mybir as mybir
from concourse.bass_utils import run_bass_kernel_spmd

F32 = mybir.dt.float32
BF16 = mybir.dt.bfloat16
AF = mybir.ActivationFunctionType
ALU = mybir.AluOpType
AX = mybir.AxisListType.X

B, T, D = 4, 4096, 1024
H, DH = 16, 64
NP = 8
OWN = 2048
NBLK, BLK = 8, 512
TT2 = 256
DFF = 4096
EPS_LN = 1e-5

ENGINES = ("sync", "vector", "scalar", "tensor", "gpsimd")


class Ctr:
    def __init__(self, nc, name):
        self.h = nc.alloc_semaphore(name)
        self.v = 0

    def bump(self, n=1):
        self.v += n
        return (self.h, n)


class Rec:

    def __init__(self):
        self.q = {e: [] for e in ENGINES}

    def op(self, eng, meth, *a, inc=None, **kw):
        self.q[eng].append(("op", meth, a, kw, inc))

    def vedge(self, ctr):
        assert self.q["vector"][-1][4] is None, "vedge would clobber an inc"
        self.q["vector"][-1] = self.q["vector"][-1][:4] + (ctr.bump(),)
        self.wait("vector", ctr)

    def wait(self, eng, ctr, val=None):
        v = ctr.v if val is None else val
        if v > 0:
            self.q[eng].append(("wait", ctr.h, v))

    def emit(self, nc):
        with nc.allow_low_precision(reason="bf16 matmul operands with fp32 accumulation (sanctioned)"), \
             nc.Block() as block:
            for ename in ENGINES:
                items = self.q[ename]
                if not items:
                    continue

                def mk(items=items):
                    def body(e):
                        for it in items:
                            if it[0] == "wait":
                                e.wait_ge(it[1], it[2])
                            else:
                                _, meth, a, kw, inc = it
                                rr = getattr(e, meth)(*a, **kw)
                                if inc is not None:
                                    rr.then_inc(inc[0], inc[1])
                    return body

                getattr(block, ename)(mk())


def build_program():
    nc = bass.Bass("TRN2", target_bir_lowering=False, detect_race_conditions=not globals().get("_RACECHECK", False) and False or globals().get("_RACECHECK", False))

    x_d = nc.dram_tensor("x", [T, D], F32, kind="ExternalInput").ap()
    wqkv_d = nc.dram_tensor("wqkv", [D, 3 * D], F32, kind="ExternalInput").ap()
    wout_d = nc.dram_tensor("wout", [D, D], F32, kind="ExternalInput").ap()
    bout_d = nc.dram_tensor("bout", [1, D], F32, kind="ExternalInput").ap()
    wfc_d = nc.dram_tensor("wfc", [DFF, D], F32, kind="ExternalInput").ap()
    lnw_d = nc.dram_tensor("lnw", [128, 32], F32, kind="ExternalInput").ap()
    lnb_d = nc.dram_tensor("lnb", [128, 32], F32, kind="ExternalInput").ap()
    wproj_d = nc.dram_tensor("wproj", [D, DFF], F32, kind="ExternalInput").ap()
    cos_hi_d = nc.dram_tensor("cos_hi", [128, T], F32, kind="ExternalInput").ap()
    cos_lo_d = nc.dram_tensor("cos_lo", [128, T], F32, kind="ExternalInput").ap()
    sin_hi_d = nc.dram_tensor("sin_hi", [128, T], F32, kind="ExternalInput").ap()
    sin_lo_d = nc.dram_tensor("sin_lo", [128, T], F32, kind="ExternalInput").ap()
    out_d = nc.dram_tensor("out", [OWN, D], F32, kind="ExternalOutput").ap()

    _scratch_kind = "ExternalOutput" if globals().get("_DBG_IO") else "Internal"
    xn_s = nc.dram_tensor("xn_s", [128, 8, OWN], BF16, kind=_scratch_kind).ap()
    k_s = nc.dram_tensor("k_s", [128, NP, T], BF16, kind=_scratch_kind).ap()
    q_s = nc.dram_tensor("q_s", [128, NP, OWN], BF16, kind=_scratch_kind).ap()
    v_s = nc.dram_tensor("v_s", [128, NP, 32 * 256], BF16, kind=_scratch_kind).ap()

    ident_np = np.eye(128, dtype=ml_dtypes.bfloat16)
    pm = np.zeros((128, 128), np.float32)
    for base in (0, 64):
        for m in range(32):
            pm[base + m + 32, base + m] = -1.0
            pm[base + m, base + m + 32] = 1.0
    ident_bf = nc.inline_tensor(ident_np, "c_ident").ap()
    pmat_bf = nc.inline_tensor(pm.astype(ml_dtypes.bfloat16), "c_pmat").ap()
    ones_bf = nc.inline_tensor(np.ones((128, 128), dtype=ml_dtypes.bfloat16), "c_ones").ap()
    ones_f32 = nc.inline_tensor(np.ones((128, 128), dtype=np.float32), "c_onesf").ap()
    c64_bf = nc.inline_tensor(np.full((128, 128), 1.0 / 64, dtype=ml_dtypes.bfloat16), "c_c64").ap()

    ges = ExitStack()

    def gsb(name, shape, dt):
        return ges.enter_context(nc.sbuf_tensor(name, shape, dt))

    g_ident = gsb("g_ident", [128, 128], BF16)
    g_pmat = gsb("g_pmat", [128, 128], BF16)
    g_ones_bf = gsb("g_ones_bf", [128, 128], BF16)
    g_ones_f = gsb("g_ones_f", [128, 128], F32)
    g_c64 = gsb("g_c64", [128, 128], BF16)
    g_lnw = gsb("g_lnw", [128, 32], F32)
    g_lnb = gsb("g_lnb", [128, 32], F32)
    g_brow = gsb("g_brow", [1, D], BF16)
    wc_buf = [gsb(f"wc_buf{i}", [128, 1024], F32) for i in range(4)]
    wqn_buf = [gsb(f"wqn_buf{i}", [128, 1024], BF16) for i in range(2)]
    red96 = gsb("red96", [128, 384], F32)
    acc1 = gsb("acc1", [128, 1], F32)
    thr_fc = gsb("thr_fc", [128, 2], F32)
    thr_pj = gsb("thr_pj", [128, 2], F32)
    thr11 = gsb("thr11", [1, 1], F32)

    C = lambda n: Ctr(nc, n)
    s_const = C("s_const")
    s_cg = C("s_cg")
    s_x = C("s_x")
    s_tab = [[C(f"s_tab{p}{w}") for w in range(2)] for p in range(2)]
    s_wq = C("s_wq")
    s_w2 = [C("s_w2a"), C("s_w2b"), C("s_w2c"), C("s_w2d")]
    s_xn2 = [C("s_xn2a"), C("s_xn2b")]
    s_kvq = [C("s_kvqa"), C("s_kvqb")]
    s_res = [C("s_resa"), C("s_resb")]
    s_w4 = C("s_w4")
    s_fin = C("s_fin")
    st_k = [C("st_ka"), C("st_kb")]
    st_q = [C("st_qa"), C("st_qb")]
    st_v = [C("st_va"), C("st_vb")]
    st_xn = [C("st_xna"), C("st_xnb")]
    st_res = [C("st_resa"), C("st_resb")]
    st_out = [C("st_outa"), C("st_outb")]
    c_t = C("c_t")
    c_v = C("c_v")
    c_s = C("c_s")
    c_qk = [C("c_qkE"), C("c_qkO")]
    c_ex = [C("c_exE"), C("c_exO")]
    c_pv = [C("c_pvE"), C("c_pvO")]
    c_fin = C("c_fin")
    c_e = C("c_e")
    c_fbb = C("c_fbb")
    c_ftt = C("c_ftt")

    r = Rec()
    r.op("sync", "dma_start", out=g_ident[:], in_=ident_bf, inc=s_const.bump(16))
    r.op("sync", "dma_start", out=g_pmat[:], in_=pmat_bf, inc=s_const.bump(16))
    r.op("sync", "dma_start", out=g_ones_bf[:], in_=ones_bf, inc=s_const.bump(16))
    r.op("sync", "dma_start", out=g_ones_f[:], in_=ones_f32, inc=s_const.bump(16))
    r.op("sync", "dma_start", out=g_c64[:], in_=c64_bf, inc=s_const.bump(16))
    r.op("sync", "dma_start", out=g_lnw[:], in_=lnw_d, inc=s_const.bump(16))
    r.op("sync", "dma_start", out=g_lnb[:], in_=lnb_d, inc=s_const.bump(16))
    r.op("gpsimd", "dma_start", out=g_brow[:], in_=bout_d, inc=s_cg.bump(16))
    r.emit(nc)

    es1 = ExitStack()

    def sb1(name, shape, dt):
        return es1.enter_context(nc.sbuf_tensor(name, shape, dt))

    ctab = [[sb1(f"ctab{p}{w}", [128, BLK], F32) for w in range(2)] for p in range(2)]
    lo_sml = [sb1(f"lo_sml{p}", [128, BLK], F32) for p in range(2)]
    wqkv_bf = sb1("wqkv_bf", [128, 8 * 3 * D], BF16)
    x_t = [sb1(f"x_t{i}", [128, D], F32) for i in range(4)]
    xn_tm = [sb1(f"xn_tm{i}", [128, D], BF16) for i in range(4)]
    xn_fm = [sb1(f"xn_fm{i}", [128, 8 * BLK], BF16) for i in range(2)]
    k_blk = [sb1(f"k_blk{i}", [128, NP * BLK], BF16) for i in range(2)]
    q_blk = [sb1(f"q_blk{i}", [128, NP * BLK], BF16) for i in range(2)]
    v_blk = [sb1(f"v_blk{i}", [128, NP * 4 * 256], BF16) for i in range(2)]
    kbf = [sb1(f"kbf{i}", [128, BLK], BF16) for i in range(2)]
    t1_scr = sb1("t1_scr", [128, BLK], F32)
    t2_scr = sb1("t2_scr", [128, BLK], F32)
    sq_scr = [sb1(f"sq_scr{i}", [128, D], BF16) for i in range(4)]
    st_a = sb1("st_a", [128, 4], F32)
    st_b = sb1("st_b", [128, 4], F32)
    st_c = sb1("st_c", [128, 4], F32)
    st_r = sb1("st_r", [128, 4], F32)
    st_nb = sb1("st_nb", [128, 4], F32)

    pK = [es1.enter_context(nc.psum_tensor(f"pK{i}", [128, BLK], F32)) for i in range(2)]
    pR = [es1.enter_context(nc.psum_tensor(f"pR{i}", [128, BLK], F32)) for i in range(2)]
    pV = [es1.enter_context(nc.psum_tensor(f"pV{i}", [128, BLK], F32)) for i in range(2)]
    pT1 = [es1.enter_context(nc.psum_tensor(f"pT1{i}", [128, BLK], BF16)) for i in range(2)]

    r = Rec()
    tab_free = [0, 0]
    tab_ready = [0, 0]

    def load_tables(blk):
        par = blk % 2
        tok0 = blk * BLK
        for w, hi_d, lo_d in ((0, cos_hi_d, cos_lo_d), (1, sin_hi_d, sin_lo_d)):
            r.wait("sync", c_v, tab_free[par] if w == 0 else c_v.v)
            r.op("sync", "dma_start", out=ctab[par][w][:],
                 in_=hi_d[:, tok0:tok0 + BLK], inc=s_tab[par][w].bump(16))
            r.op("sync", "dma_start", out=lo_sml[par][:],
                 in_=lo_d[:, tok0:tok0 + BLK], inc=s_tab[par][w].bump(16))
            r.wait("vector", s_tab[par][w], s_tab[par][w].v)
            r.op("vector", "tensor_add", ctab[par][w][:], ctab[par][w][:],
                 lo_sml[par][:], inc=c_v.bump())
        tab_ready[par] = c_v.v

    load_tables(0)
    r.op("gpsimd", "dma_start", out=wqkv_bf[:],
         in_=wqkv_d.rearrange("(c p) n -> p c n", p=128), inc=s_wq.bump(16))
    r.wait("tensor", s_const)
    r.wait("vector", s_const)
    r.wait("scalar", s_const)

    wfc_rows_g = wfc_d.rearrange("(c p) d -> c p d", p=128)
    wpj_rows_g = wproj_d.rearrange("(c p) (u w) -> c u p w", p=128, w=1024)
    gamma_jobs = [wfc_rows_g[i] for i in range(32)] + \
                 [wpj_rows_g[ci, ui] for ci in range(8) for ui in range(4)]
    gamma_ct = [0]
    gamma_marks = {}

    def emit_gamma(njobs):
        for _ in range(njobs):
            ch = gamma_ct[0]
            if ch >= len(gamma_jobs):
                return
            bi = ch % 4
            if ch - 4 in gamma_marks:
                r.wait("sync", c_v, gamma_marks[ch - 4])
            r.op("sync", "dma_start", out=wc_buf[bi][:], in_=gamma_jobs[ch],
                 inc=s_w2[bi].bump(16))
            r.wait("vector", s_w2[bi], s_w2[bi].v)
            r.op("vector", "tensor_reduce",
                 red96[:, ch * 4:(ch + 1) * 4].rearrange("p (s u) -> p s u", u=1),
                 wc_buf[bi][:].rearrange("p (s u) -> p s u", u=256),
                 AX, ALU.add, apply_absolute_value=True, inc=c_v.bump())
            gamma_marks[ch] = c_v.v
            gamma_ct[0] += 1

    pk_last, pr_last, kbf_last = {}, {}, {}

    cv_xnfm_mark = [0, 0]

    def emit_lnstats(blk):
        nonlocal_vars = None
        tok0 = blk * BLK
        sxb = s_x.v
        r.wait("sync", c_s, ln_state["mark_xload_s"])
        r.wait("sync", c_v, ln_state["mark_xload_v"])
        for tt in range(4):
            r.op("sync", "dma_start", out=x_t[tt][:],
                 in_=x_d[tok0 + tt * 128: tok0 + (tt + 1) * 128, :], inc=s_x.bump(16))
        if blk >= 1:
            load_tables(blk)
        emit_gamma(8)
        r.wait("scalar", c_v, ln_state["prev_cv_stats"])
        for tt in range(4):
            r.wait("vector", s_x, sxb + 64)
            r.op("vector", "reduce_sum", st_a[:, tt:tt + 1], x_t[tt][:], AX,
                 inc=(c_e.bump() if tt == 3 else c_v.bump()))
            if tt == 3:
                r.wait("vector", c_e)
                r.op("vector", "engine_nop", inc=c_v.bump())
            r.wait("scalar", s_x, sxb + 64)
            r.op("scalar", "activation", sq_scr[tt][:], x_t[tt][:], AF.Square,
                 accum_out=st_b[:, tt:tt + 1], inc=c_s.bump())
        ln_state["mark_xload_v"] = c_v.v
        r.wait("vector", c_s)

        def vedge():
            r.q["vector"][-1] = r.q["vector"][-1][:4] + (c_e.bump(),)
            r.wait("vector", c_e)

        r.op("vector", "tensor_scalar_mul", st_a[:], st_a[:], 1.0 / D)
        vedge()
        r.op("vector", "tensor_scalar_mul", st_b[:], st_b[:], 1.0 / D)
        vedge()
        r.op("vector", "tensor_mul", st_c[:], st_a[:], st_a[:])
        vedge()
        r.op("vector", "tensor_sub", st_b[:], st_b[:], st_c[:])
        vedge()
        r.op("vector", "tensor_scalar_add", st_b[:], st_b[:], EPS_LN, inc=c_v.bump())
        r.wait("scalar", c_v)
        r.op("scalar", "activation", st_c[:], st_b[:], AF.Sqrt, inc=c_s.bump())
        r.wait("vector", c_s)
        r.op("vector", "reciprocal", st_r[:], st_c[:])
        vedge()
        r.op("vector", "tensor_mul", st_c[:], st_r[:], st_r[:])
        vedge()
        r.op("vector", "tensor_mul", st_c[:], st_c[:], st_b[:])
        vedge()
        r.op("vector", "tensor_scalar", st_c[:], st_c[:], -0.5, 1.5, ALU.mult, ALU.add)
        vedge()
        r.op("vector", "tensor_mul", st_r[:], st_r[:], st_c[:])
        vedge()
        r.op("vector", "tensor_mul", st_nb[:], st_a[:], st_r[:])
        vedge()
        r.op("vector", "tensor_scalar_mul", st_nb[:], st_nb[:], -1.0, inc=c_v.bump())
        cv_stats = c_v.v
        ln_state["prev_cv_stats"] = cv_stats
        r.wait("scalar", c_v, cv_stats)
        r.wait("scalar", c_t, ln_state["mark_tp"])
        for tt in range(4):
            r.op("scalar", "activation", xn_tm[tt][:], x_t[tt][:], AF.Identity,
                 bias=st_nb[:, tt:tt + 1], scale=st_r[:, tt:tt + 1], inc=c_s.bump())
        ln_state["mark_xload_s"] = c_s.v

    def emit_transposes(blk):
        bi = blk % 2
        r.wait("tensor", c_s, ln_state["mark_xload_s"])
        r.wait("tensor", st_xn[bi], st_xn[bi].v)
        r.wait("tensor", c_t, ln_state["mark_xnfm_free"][bi])
        r.wait("tensor", c_v)
        for d in range(8):
            if d >= 2:
                r.wait("tensor", c_v)
            for tt in range(4):
                r.op("tensor", "transpose", pT1[d % 2][:, tt * 128:(tt + 1) * 128],
                     xn_tm[tt][:, d * 128:(d + 1) * 128], g_ident[:],
                     inc=(c_t.bump() if tt == 3 else None))
            r.wait("vector", c_t)
            r.op("vector", "tensor_copy", xn_fm[bi][:, d * BLK:(d + 1) * BLK],
                 pT1[d % 2][:], inc=c_v.bump())
        ln_state["mark_tp"] = c_t.v
        cv_xnfm_mark[bi] = c_v.v
        if blk < 4:
            tok0 = blk * BLK
            r.wait("gpsimd", c_v, cv_xnfm_mark[bi])
            r.op("gpsimd", "dma_start", out=xn_s[:, :, tok0:tok0 + BLK],
                 in_=xn_fm[bi][:].rearrange("p (d t) -> p d t", t=BLK),
                 inc=st_xn[bi].bump(16))

    def emit_qkv(blk):
        own_blk = blk < 4
        tok0 = blk * BLK
        bi = blk % 2
        r.wait("tensor", s_wq, 16)
        r.wait("tensor", c_v, cv_xnfm_mark[bi])

        def proj_rope(col0, buf, jbase):
            ct_proj = {}
            cv_kbf = {}
            for p in range(NP + 1):
                if p < NP:
                    j = jbase + p
                    pk = pK[j % 2]
                    if j % 2 in pk_last:
                        r.wait("tensor", c_v, pk_last[j % 2])
                    for d in range(8):
                        r.op("tensor", "matmul", pk[:],
                             wqkv_bf[:, d * 3 * D + col0 + p * 128: d * 3 * D + col0 + (p + 1) * 128],
                             xn_fm[bi][:, d * BLK:(d + 1) * BLK],
                             start=(d == 0), stop=(d == 7),
                             inc=(c_t.bump() if d == 7 else None))
                    ct_proj[p] = c_t.v
                    if j % 2 in kbf_last:
                        r.wait("vector", c_t, kbf_last[j % 2])
                    r.wait("vector", c_t, ct_proj[p])
                    r.op("vector", "tensor_copy", kbf[j % 2][:], pk[:], inc=c_v.bump())
                    cv_kbf[p] = c_v.v
                if p >= 1:
                    pm1 = p - 1
                    jm = jbase + pm1
                    pkm, prm, kbm = pK[jm % 2], pR[jm % 2], kbf[jm % 2]
                    r.wait("tensor", c_v, cv_kbf[pm1])
                    if jm % 2 in pr_last:
                        r.wait("tensor", c_v, pr_last[jm % 2])
                    r.op("tensor", "matmul", prm[:], g_pmat[:], kbm[:],
                         start=True, stop=True, inc=c_t.bump())
                    kbf_last[jm % 2] = c_t.v
                    ct_rot = c_t.v
                    r.op("vector", "tensor_mul", t1_scr[:], pkm[:],
                         ctab[blk % 2][0][:], inc=c_v.bump())
                    pk_last[jm % 2] = c_v.v
                    r.wait("vector", c_t, ct_rot)
                    r.op("vector", "tensor_mul", t2_scr[:], prm[:],
                         ctab[blk % 2][1][:], inc=c_v.bump())
                    pr_last[jm % 2] = c_v.v
                    r.op("vector", "tensor_add", buf[:, pm1 * BLK:(pm1 + 1) * BLK],
                         t1_scr[:], t2_scr[:], inc=c_v.bump())

        r.wait("vector", st_k[bi], st_k[bi].v)
        proj_rope(D, k_blk[bi], blk * 24)
        r.wait("gpsimd", c_v)
        r.op("gpsimd", "dma_start", out=k_s[:, :, tok0:tok0 + BLK],
             in_=k_blk[bi][:].rearrange("p (n t) -> p n t", t=BLK),
             inc=st_k[bi].bump(16))
        if own_blk:
            r.wait("vector", st_q[bi], st_q[bi].v)
            proj_rope(0, q_blk[bi], blk * 24 + 8)
            r.wait("gpsimd", c_v)
            r.op("gpsimd", "dma_start", out=q_s[:, :, tok0:tok0 + BLK],
                 in_=q_blk[bi][:].rearrange("p (n t) -> p n t", t=BLK),
                 inc=st_q[bi].bump(16))
        vb = v_blk[bi]
        r.wait("vector", st_v[bi], st_v[bi].v)
        if blk < 2:
            r.op("vector", "memset",
                 vb[:].rearrange("p (k c) -> p k c", c=256)[:, :, 64:192], 1.0,
                 inc=c_v.bump())
        r.wait("tensor", c_v)
        for tt in range(4):
            for cc in range(2):
                jv = tt * 2 + cc
                pv = pV[jv % 2]
                if jv >= 2:
                    r.wait("tensor", c_v)
                for d in range(8):
                    r.op("tensor", "matmul", pv[:],
                         xn_fm[bi][:, d * BLK + tt * 128: d * BLK + (tt + 1) * 128],
                         wqkv_bf[:, d * 3 * D + 2 * D + cc * 512: d * 3 * D + 2 * D + (cc + 1) * 512],
                         start=(d == 0), stop=(d == 7),
                         inc=(c_t.bump() if d == 7 else None))
                r.wait("vector", c_t)
                srcv = pv[:].rearrange("p (n j c) -> p n j c", j=2, c=64)
                vbv = vb[:].rearrange("p (n k c) -> p n k c", k=4, c=256)
                r.op("vector", "tensor_copy",
                     vbv[:, cc * 4:(cc + 1) * 4, tt, 0:64], srcv[:, :, 0, :])
                r.op("vector", "tensor_copy",
                     vbv[:, cc * 4:(cc + 1) * 4, tt, 192:256], srcv[:, :, 1, :],
                     inc=c_v.bump())
        ln_state["mark_xnfm_free"][bi] = c_t.v
        tab_free[blk % 2] = c_v.v
        r.wait("gpsimd", c_v)
        r.op("gpsimd", "dma_start",
             out=v_s[:, :, blk * 4 * 256:(blk + 1) * 4 * 256],
             in_=vb[:].rearrange("p (n c) -> p n c", c=4 * 256),
             inc=st_v[bi].bump(16))

    ln_state = {"mark_xload_s": 0, "mark_xload_v": c_v.v, "prev_cv_stats": 0,
                "mark_xnfm_free": [0, 0], "mark_tp": 0}
    emit_lnstats(0)
    emit_transposes(0)
    for blk in range(NBLK):
        if blk + 1 < NBLK:
            emit_lnstats(blk + 1)
        emit_qkv(blk)
        if blk + 1 < NBLK:
            emit_transposes(blk + 1)
    emit_gamma(len(gamma_jobs))

    def finish_thr(cols, thr):
        r.op("vector", "tensor_reduce", acc1[:], red96[:, cols[0] * 4:cols[1] * 4],
             AX, ALU.add, inc=c_e.bump())
        r.wait("vector", c_e)
        r.op("vector", "engine_nop", inc=c_v.bump())
        r.wait("tensor", c_v)
        for par in (0, 1):
            if par in pr_last:
                r.wait("tensor", c_v, pr_last[par])
        r.op("tensor", "matmul", pR[0][0:1, 0:1], g_ones_f[:, 0:1], acc1[:],
             start=True, stop=True, inc=c_t.bump())
        r.wait("vector", c_t)
        r.op("vector", "tensor_scalar", thr11[:], pR[0][0:1, 0:1], 0.5 / (DFF * D),
             0.5e-6, ALU.mult, ALU.add, inc=c_v.bump())
        r.wait("tensor", c_v)
        r.op("tensor", "matmul", pR[0][:, 0:1], g_ones_f[0:1, 0:128], thr11[:],
             start=True, stop=True, inc=c_t.bump())
        r.wait("vector", c_t)
        r.op("vector", "tensor_copy", thr[:, 0:1], pR[0][:, 0:1])
        r.op("vector", "tensor_scalar_mul", thr[:, 1:2], pR[0][:, 0:1], -1.0,
             inc=c_v.bump())

    finish_thr((0, 32), thr_fc)
    finish_thr((32, 64), thr_pj)
    r.op("vector", "engine_nop", inc=c_e.bump())
    r.wait("vector", c_e)
    r.op("vector", "engine_nop", inc=c_v.bump())
    cv_thr = c_v.v

    p1_end = dict(c_v=c_v.v, c_s=c_s.v, c_t=c_t.v)
    p1_stores = [(st_k[0], st_k[0].v), (st_k[1], st_k[1].v),
                 (st_q[0], st_q[0].v), (st_q[1], st_q[1].v),
                 (st_v[0], st_v[0].v), (st_v[1], st_v[1].v),
                 (st_xn[0], st_xn[0].v), (st_xn[1], st_xn[1].v)]
    r.emit(nc)
    es1.close()

    es2 = ExitStack()

    def sb2(name, shape, dt):
        return es2.enter_context(nc.sbuf_tensor(name, shape, dt))

    wqfcT = sb2("wqfcT", [128, 8 * DFF], BF16)
    wqprojT = sb2("wqprojT", [128, 32 * D], BF16)
    h_sb = sb2("h_sb", [128, 32 * TT2], BF16)
    xn_t = [sb2(f"xn_t{i}", [128, 8 * TT2], BF16) for i in range(2)]
    mu_sb = sb2("mu_sb", [128, TT2], F32)
    ex2_sb = sb2("ex2_sb", [128, TT2], F32)
    var_sb = sb2("var_sb", [128, TT2], F32)
    rr_sb = sb2("rr_sb", [128, TT2], F32)
    tln_scr = sb2("tln_scr", [128, TT2], F32)
    x2_t = [sb2(f"x2_t{i}", [128, D], F32) for i in range(2)]
    out2_t = [sb2(f"out2_t{i}", [128, D], F32) for i in range(2)]

    es2a = ExitStack()
    pT2 = [es2a.enter_context(nc.psum_tensor(f"pT2{i}", [128, 512], BF16)) for i in range(2)]

    r = Rec()
    for e in ENGINES:
        for ctr, vv in p1_stores:
            r.wait(e, ctr, vv)
        r.wait(e, c_v, p1_end["c_v"])
        r.wait(e, c_s, p1_end["c_s"])
        r.wait(e, c_t, p1_end["c_t"])

    wfc_rows = wfc_d.rearrange("(c p) d -> c p d", p=128)
    wpj_rows = wproj_d.rearrange("(c p) n -> c p n", p=128)

    tq_marks = {}
    tqt_marks = {}

    def quant_transpose(load_ap, thr, ntiles, dst_fn, qi):
        bi = qi % 2
        w = load_ap.shape[-1]
        if qi - 2 in tq_marks:
            r.wait("sync", c_v, tq_marks[qi - 2])
        else:
            r.wait("sync", c_v, cv_thr)
        r.op("sync", "dma_start", out=wc_buf[bi][:, 0:w], in_=load_ap,
             inc=s_w2[bi].bump(16))
        r.wait("vector", s_w2[bi], s_w2[bi].v)
        r.wait("vector", c_v, cv_thr)
        if qi - 2 in tqt_marks:
            r.wait("vector", c_t, tqt_marks[qi - 2])
        wv = wc_buf[bi][:, 0:w]
        qn = wqn_buf[bi][:, 0:w]
        r.op("vector", "tensor_single_scalar", qn, wv, thr[:, 1:2], ALU.is_lt)
        r.op("vector", "scalar_tensor_tensor", qn, wv, thr[:, 0:1], qn,
             ALU.is_gt, ALU.subtract, inc=c_v.bump())
        tq_marks[qi] = c_v.v
        cv_q = c_v.v
        evac_marks = {}
        for grp in range(ntiles // 4):
            pt = pT2[grp % 2]
            r.wait("tensor", c_v, evac_marks.get(grp - 2, cv_q))
            for u in range(4):
                ti = grp * 4 + u
                r.op("tensor", "transpose", pt[:, u * 128:(u + 1) * 128],
                     qn[:, ti * 128:(ti + 1) * 128], g_ident[:],
                     inc=(c_t.bump() if u == 3 else None))
            r.wait("vector", c_t)
            r.op("vector", "tensor_copy", dst_fn(grp), pt[:].rearrange(
                "p (u c) -> p u c", c=128), inc=c_v.bump())
            evac_marks[grp] = c_v.v
        tqt_marks[qi] = c_t.v

    qi = 0
    wqfcT_v = wqfcT[:].rearrange("p (d h) -> p d h", h=DFF)
    for c in range(32):
        def dst_fc(grp, c=c):
            return wqfcT_v[:, 4 * grp:4 * grp + 4, c * 128:(c + 1) * 128]
        quant_transpose(wfc_rows[c], thr_fc, 8, dst_fc, qi)
        qi += 1
    wqprojT_v = wqprojT[:].rearrange("p (m d) -> p m d", d=D)
    for ci in range(8):
        for ui in range(4):
            def dst_pj(grp, ci=ci, ui=ui):
                return wqprojT_v[:, 8 * ui + 4 * grp: 8 * ui + 4 * grp + 4,
                                 ci * 128:(ci + 1) * 128]
            quant_transpose(wpj_rows[ci][:, ui * 1024:(ui + 1) * 1024], thr_pj, 8,
                            dst_pj, qi)
            qi += 1

    cv_wq_done = c_v.v
    if globals().get("_DBG_WQ"):
        _k = "ExternalOutput" if globals().get("_DBG_IO") else "Internal"
        dbg_fc = nc.dram_tensor("dbg_fc", [128, 8 * DFF], BF16, kind=_k).ap()
        dbg_pj = nc.dram_tensor("dbg_pj", [128, 32 * D], BF16, kind=_k).ap()
        r.wait("gpsimd", c_v, cv_wq_done)
        r.op("gpsimd", "dma_start", out=dbg_fc, in_=wqfcT[:], inc=st_k[0].bump(16))
        r.op("gpsimd", "dma_start", out=dbg_pj, in_=wqprojT[:], inc=st_k[0].bump(16))
    es2a.close()
    h2_ring = sb2("h2_ring", [128, 4 * TT2], BF16)
    pF = [es2.enter_context(nc.psum_tensor(f"pF{i}", [128, TT2], F32)) for i in range(2)]
    pS1 = es2.enter_context(nc.psum_tensor("pS1", [128, TT2], F32))
    pS2 = es2.enter_context(nc.psum_tensor("pS2", [128, TT2], F32))
    pP2 = [es2.enter_context(nc.psum_tensor(f"pP2{i}", [128, D], F32)) for i in range(2)]

    xn_marks = {}
    res_marks = {}
    for ti in range(OWN // TT2):
        bi = ti % 2
        if ti - 2 in xn_marks:
            r.wait("sync", c_t, xn_marks[ti - 2])
        r.op("sync", "dma_start", out=xn_t[bi][:],
             in_=xn_s[:, :, ti * TT2:(ti + 1) * TT2], inc=s_xn2[bi].bump(16))
        r.wait("tensor", s_xn2[bi], s_xn2[bi].v)
        r.wait("tensor", c_v, cv_wq_done)
        gelu_marks = {}
        h2_done = None
        for m in range(32):
            pf = pF[m % 2]
            if m - 2 in gelu_marks:
                r.wait("tensor", c_s, gelu_marks[m - 2])
            for d in range(8):
                r.op("tensor", "matmul", pf[:],
                     wqfcT[:, d * DFF + m * 128: d * DFF + (m + 1) * 128],
                     xn_t[bi][:, d * TT2:(d + 1) * TT2],
                     start=(d == 0), stop=(d == 7),
                     inc=(c_t.bump() if d == 7 else None))
            r.wait("scalar", c_t)
            r.op("scalar", "activation", h_sb[:, m * TT2:(m + 1) * TT2], pf[:],
                 AF.Gelu, inc=c_s.bump())
            gelu_marks[m] = c_s.v
        cs_gelu_all = c_s.v
        r.wait("tensor", c_s, cs_gelu_all)
        for m in range(32):
            r.op("tensor", "matmul", pS1[:], g_ones_bf[:],
                 h_sb[:, m * TT2:(m + 1) * TT2], start=(m == 0), stop=(m == 31),
                 skip_group_check=True, inc=(c_t.bump() if m == 31 else None))
        h2_marks = {}
        ps2_marks = {}
        for m in range(32):
            sl = m % 4
            r.wait("vector", c_s, gelu_marks[m])
            if m - 4 in ps2_marks:
                r.wait("vector", c_t, ps2_marks[m - 4])
            r.op("vector", "tensor_mul",
                 h2_ring[:, sl * TT2:(sl + 1) * TT2], h_sb[:, m * TT2:(m + 1) * TT2],
                 h_sb[:, m * TT2:(m + 1) * TT2], inc=c_v.bump())
            h2_marks[m] = c_v.v
            r.wait("tensor", c_v, h2_marks[m])
            r.op("tensor", "matmul", pS2[:], g_ones_bf[:],
                 h2_ring[:, sl * TT2:(sl + 1) * TT2], start=(m == 0), stop=(m == 31),
                 skip_group_check=True, inc=c_t.bump())
            ps2_marks[m] = c_t.v
        xn_marks[ti] = c_t.v
        ct_stats = c_t.v
        r.wait("vector", c_t, ct_stats)
        r.wait("vector", c_s)
        r.op("vector", "tensor_scalar_mul", mu_sb[:], pS1[:], 1.0 / DFF)
        r.op("vector", "tensor_scalar_mul", ex2_sb[:], pS2[:], 1.0 / DFF)
        r.op("vector", "tensor_mul", var_sb[:], mu_sb[:], mu_sb[:])
        r.op("vector", "tensor_sub", var_sb[:], ex2_sb[:], var_sb[:])
        r.op("vector", "tensor_scalar_add", var_sb[:], var_sb[:], EPS_LN, inc=c_v.bump())
        r.wait("scalar", c_v)
        r.op("scalar", "activation", ex2_sb[:], var_sb[:], AF.Sqrt, inc=c_s.bump())
        r.wait("vector", c_s)
        r.op("vector", "reciprocal", rr_sb[:], ex2_sb[:])
        r.op("vector", "tensor_mul", tln_scr[:], rr_sb[:], rr_sb[:])
        r.op("vector", "tensor_mul", tln_scr[:], tln_scr[:], var_sb[:])
        r.op("vector", "tensor_scalar", tln_scr[:], tln_scr[:], -0.5, 1.5, ALU.mult, ALU.add)
        r.op("vector", "tensor_mul", rr_sb[:], rr_sb[:], tln_scr[:], inc=c_v.bump())
        cv_r_done = c_v.v
        for tm in range(TT2 // 128):
            sub = ti * (TT2 // 128) + tm
            bi2 = tm
            if (ti - 1, tm) in res_marks:
                r.wait("sync", c_v, res_marks[(ti - 1, tm)])
            row0 = ti * TT2 + tm * 128
            r.op("sync", "dma_start", out=x2_t[bi2][:], in_=x_d[row0:row0 + 128, :],
                 inc=s_res[bi2].bump(16))
        for m in range(32):
            r.op("vector", "tensor_sub", tln_scr[:], h_sb[:, m * TT2:(m + 1) * TT2], mu_sb[:])
            r.op("vector", "tensor_mul", tln_scr[:], tln_scr[:], rr_sb[:])
            r.op("vector", "tensor_scalar", h_sb[:, m * TT2:(m + 1) * TT2], tln_scr[:],
                 g_lnw[:, m:m + 1], g_lnb[:, m:m + 1], ALU.mult, ALU.add,
                 inc=c_v.bump())
            cv_norm_m = c_v.v
            if m == 0:
                for tm in range(TT2 // 128):
                    if (ti - 1, tm) in res_marks:
                        r.wait("tensor", c_v, res_marks[(ti - 1, tm)])
            r.wait("tensor", c_v, cv_norm_m)
            for tm in range(TT2 // 128):
                for n in range(2):
                    r.op("tensor", "matmul", pP2[tm][:, n * 512:(n + 1) * 512],
                         h_sb[:, m * TT2 + tm * 128: m * TT2 + (tm + 1) * 128],
                         wqprojT[:, m * D + n * 512: m * D + (n + 1) * 512],
                         start=(m == 0), stop=(m == 31), skip_group_check=True,
                         inc=(c_t.bump() if (m == 31 and n == 1 and tm == TT2 // 128 - 1)
                              else None))
        ct_proj = c_t.v
        for tm in range(TT2 // 128):
            sub = ti * (TT2 // 128) + tm
            bi2 = tm
            row0 = ti * TT2 + tm * 128
            r.wait("vector", c_t, ct_proj)
            r.wait("vector", s_res[bi2], s_res[bi2].v)
            r.wait("vector", st_res[bi2], st_res[bi2].v)
            r.op("vector", "tensor_add", out2_t[bi2][:], pP2[tm][:], x2_t[bi2][:],
                 inc=c_v.bump())
            res_marks[(ti, tm)] = c_v.v
            r.wait("gpsimd", c_v, res_marks[(ti, tm)])
            r.op("gpsimd", "dma_start", out=out_d[row0:row0 + 128, :],
                 in_=out2_t[bi2][:], inc=st_res[bi2].bump(16))

    p2_end = dict(c_v=c_v.v, c_s=c_s.v, c_t=c_t.v)
    p2_stores = [(st_res[0], st_res[0].v), (st_res[1], st_res[1].v)]
    r.emit(nc)
    es2.close()

    es34 = ExitStack()
    attn_all = [es34.enter_context(nc.sbuf_tensor(f"attn_all{i}", [128, OWN], BF16))
                for i in range(NP)]
    es3 = ExitStack()

    def sb3(name, shape, dt):
        return es3.enter_context(nc.sbuf_tensor(name, shape, dt))

    kb = [sb3(f"kb{i}", [128, T], BF16) for i in range(2)]
    qb = [sb3(f"qb{i}", [128, OWN], BF16) for i in range(2)]
    kb2 = [sb3(f"kb2{i}", [128, T], BF16) for i in range(2)]
    qb2 = [sb3(f"qb2{i}", [128, OWN], BF16) for i in range(2)]
    vb3 = [sb3(f"vb3{i}", [128, 32 * 256], BF16) for i in range(2)]
    probs = [[sb3(f"probs{s}{i}", [128, 1024], BF16) for i in range(2)] for s in range(1)]
    recip_sb = sb3("recip_sb", [128, 1024], BF16)
    bb_sb = sb3("bb_sb", [128, 1024], F32)
    pvsum_sb = sb3("pvsum_sb", [128, 1024], F32)

    pQK = [es3.enter_context(nc.psum_tensor(f"pQK{i}", [128, 1024], F32)) for i in range(2)]
    pPVa = es3.enter_context(nc.psum_tensor("pPVa", [128, 1024], F32))
    pPVb = es3.enter_context(nc.psum_tensor("pPVb", [128, 1024], F32))

    r = Rec()
    for e in ENGINES:
        for ctr, vv in p2_stores:
            r.wait(e, ctr, vv)
        r.wait(e, c_v, p2_end["c_v"])
        r.wait(e, c_s, p2_end["c_s"])
        r.wait(e, c_t, p2_end["c_t"])

    fin_mark = {}
    kvq_mark = {}
    fin_count = 0

    def emit_kvq_load(p):
        bb = p % 2
        if p - 2 in fin_mark:
            r.wait("sync", c_pv[0], fin_mark[p - 2])
        r.op("sync", "dma_start", out=kb[bb][:], in_=k_s[:, p, :],
             inc=s_kvq[bb].bump(16))
        r.op("sync", "dma_start", out=qb[bb][:], in_=q_s[:, p, :],
             inc=s_kvq[bb].bump(16))
        r.op("sync", "dma_start", out=vb3[bb][:], in_=v_s[:, p, :],
             inc=s_kvq[bb].bump(16))
        r.op("sync", "dma_start", out=kb2[bb][64:128, :], in_=k_s[0:64, p, :],
             inc=s_kvq[bb].bump(16))
        r.op("sync", "dma_start", out=kb2[bb][0:64, :], in_=k_s[64:128, p, :],
             inc=s_kvq[bb].bump(16))
        r.op("sync", "dma_start", out=qb2[bb][64:128, :], in_=q_s[0:64, p, :],
             inc=s_kvq[bb].bump(16))
        r.op("sync", "dma_start", out=qb2[bb][0:64, :], in_=q_s[64:128, p, :],
             inc=s_kvq[bb].bump(16))
        kvq_mark[p] = s_kvq[bb].v

    for pi in range(NP):
        bib = pi % 2
        if pi == 0:
            emit_kvq_load(0)
        if pi + 1 < NP:
            emit_kvq_load(pi + 1)
        for qc in range(2):
            for side in range(2):
                r0 = 64 * side
                d0, d1 = (64, 128) if side == 0 else (0, 64)
                v0, v1 = (0, 64) if side == 0 else (64, 128)
                e0, q0, p0 = c_ex[0].v, c_qk[0].v, c_pv[0].v

                for kt in range(32):
                    pq = pQK[kt % 2]
                    if kt == 0:
                        r.wait("tensor", s_kvq[bib], kvq_mark[pi])
                    r.wait("tensor", c_ex[0], e0 + kt - 1)
                    if kt % 2 == 0:
                        ksrc, qsrc, rr0 = kb[bib], qb[bib], r0
                    else:
                        ksrc, qsrc, rr0 = kb2[bib], qb2[bib], 64 - r0
                    for n in range(2):
                        r.op("tensor", "matmul", pq[:, n * 512:(n + 1) * 512],
                             ksrc[rr0:rr0 + 64, kt * 128:(kt + 1) * 128],
                             qsrc[rr0:rr0 + 64, qc * 1024 + n * 512: qc * 1024 + (n + 1) * 512],
                             start=True, stop=True, tile_position=(rr0, 0),
                             inc=(c_qk[0].bump() if n == 1 else None))
                    r.wait("scalar", c_qk[0], q0 + kt + 1)
                    r.wait("scalar", c_pv[0], p0 + kt - 1)
                    r.op("scalar", "activation", probs[0][kt % 2][:], pq[:],
                         AF.Exp, scale=0.125, inc=c_ex[0].bump())
                    if kt >= 1:
                        if kt == 1:
                            r.wait("tensor", c_ftt)
                        r.wait("tensor", c_ex[0], e0 + kt)
                        kc = (kt - 1) * 256 + side * 128
                        for n in range(2):
                            r.op("tensor", "matmul", pPVa[:, n * 512:(n + 1) * 512],
                                 vb3[bib][0:64, kc:kc + 128],
                                 probs[0][(kt - 1) % 2][0:64, n * 512:(n + 1) * 512],
                                 start=(kt == 1), stop=False, skip_group_check=True,
                                 tile_position=(0, 0))
                            r.op("tensor", "matmul", pPVb[:, n * 512:(n + 1) * 512],
                                 vb3[bib][64:128, kc:kc + 128],
                                 probs[0][(kt - 1) % 2][64:128, n * 512:(n + 1) * 512],
                                 start=(kt == 1), stop=False, skip_group_check=True,
                                 tile_position=(64, 0),
                                 inc=(c_pv[0].bump() if n == 1 else None))
                r.wait("tensor", c_ex[0], e0 + 32)
                kc = 31 * 256 + side * 128
                for n in range(2):
                    r.op("tensor", "matmul", pPVa[:, n * 512:(n + 1) * 512],
                         vb3[bib][0:64, kc:kc + 128],
                         probs[0][31 % 2][0:64, n * 512:(n + 1) * 512],
                         start=False, stop=True, skip_group_check=True,
                         tile_position=(0, 0))
                    r.op("tensor", "matmul", pPVb[:, n * 512:(n + 1) * 512],
                         vb3[bib][64:128, kc:kc + 128],
                         probs[0][31 % 2][64:128, n * 512:(n + 1) * 512],
                         start=False, stop=True, skip_group_check=True,
                         tile_position=(64, 0),
                         inc=(c_pv[0].bump() if n == 1 else None))
                r.wait("vector", c_pv[0])
                r.op("vector", "tensor_copy", pvsum_sb[:], pPVa[:])
                r.op("vector", "tensor_add", pvsum_sb[:], pvsum_sb[:], pPVb[:],
                     inc=c_ftt.bump())
                r.op("vector", "reciprocal", recip_sb[d0:d1, :], pvsum_sb[d0:d1, :],
                     inc=c_v.bump())
                r.wait("sync", c_v)
                r.op("sync", "dma_start", out=recip_sb[v0:v1, :],
                     in_=recip_sb[d0:d1, :], inc=s_fin.bump(16))
                r.wait("vector", s_fin, s_fin.v)
                r.op("vector", "tensor_mul",
                     attn_all[pi][v0:v1, qc * 1024:(qc + 1) * 1024],
                     pvsum_sb[v0:v1, :], recip_sb[v0:v1, :], inc=c_v.bump())
                fin_count += 1
        fin_mark[pi] = c_pv[0].v

    p3_end = dict(c_fin=c_fin.v, c_v=c_v.v, c_t=c_t.v)
    r.emit(nc)
    es3.close()

    es4 = ExitStack()

    def sb4(name, shape, dt):
        return es4.enter_context(nc.sbuf_tensor(name, shape, dt))

    wout_bf = sb4("wout_bf", [128, 8 * D], BF16)
    res_t = [sb4(f"res_t{i}", [128, D], F32) for i in range(2)]
    out4_t = [sb4(f"out4_t{i}", [128, D], F32) for i in range(2)]
    pP4 = [es4.enter_context(nc.psum_tensor(f"pP4{i}", [128, D], F32)) for i in range(2)]

    r = Rec()
    for e in ENGINES:
        r.wait(e, c_fin, p3_end["c_fin"])
        r.wait(e, c_v, p3_end["c_v"])
        r.wait(e, c_t, p3_end["c_t"])
    r.op("gpsimd", "dma_start", out=wout_bf[:],
         in_=wout_d.rearrange("(r p) n -> p r n", p=128), inc=s_w4.bump(16))
    out_marks = {}
    for ctr, vv in p2_stores:
        r.wait("sync", ctr, vv)
    for tm in range(OWN // 128):
        bi4 = tm % 2
        if tm - 2 in out_marks:
            r.wait("sync", c_v, out_marks[tm - 2])
        r.op("sync", "dma_start", out=res_t[bi4][:],
             in_=out_d[tm * 128:(tm + 1) * 128, :], inc=s_res[bi4].bump(16))
        r.wait("tensor", s_w4, 16)
        r.wait("tensor", s_const)
        r.wait("tensor", s_cg, 16)
        if tm - 2 in out_marks:
            r.wait("tensor", c_v, out_marks[tm - 2])
        for n in range(2):
            for pi in range(NP):
                r.op("tensor", "matmul", pP4[bi4][:, n * 512:(n + 1) * 512],
                     attn_all[pi][:, tm * 128:(tm + 1) * 128],
                     wout_bf[:, pi * D + n * 512: pi * D + (n + 1) * 512],
                     start=(pi == 0), stop=False)
            r.op("tensor", "matmul", pP4[bi4][:, n * 512:(n + 1) * 512],
                 g_ones_bf[0:1, 0:128], g_brow[0:1, n * 512:(n + 1) * 512],
                 start=False, stop=True,
                 inc=(c_t.bump() if n == 1 else None))
        r.wait("vector", c_t)
        r.wait("vector", s_res[bi4], s_res[bi4].v)
        r.wait("vector", st_out[bi4], st_out[bi4].v)
        r.op("vector", "tensor_add", out4_t[bi4][:], pP4[bi4][:], res_t[bi4][:],
             inc=c_v.bump())
        out_marks[tm] = c_v.v
        r.wait("gpsimd", c_v, out_marks[tm])
        r.op("gpsimd", "dma_start", out=out_d[tm * 128:(tm + 1) * 128, :],
             in_=out4_t[bi4][:], inc=st_out[bi4].bump(16))
    r.wait("gpsimd", st_out[0], st_out[0].v)
    r.wait("gpsimd", st_out[1], st_out[1].v)
    r.emit(nc)
    es4.close()
    es34.close()
    ges.close()
    return nc


_NC_CACHE = [None]


def kernel(L0_attn_x, L0_attn_w_qkv, L0_attn_w_out, L0_attn_b_out,
           L0_ffn_W_fc, L0_ffn_ln_w, L0_ffn_ln_b, L0_ffn_W_proj):
    x = np.ascontiguousarray(np.asarray(L0_attn_x, dtype=np.float32))
    wqkv = np.ascontiguousarray(np.asarray(L0_attn_w_qkv, dtype=np.float32))
    wout = np.ascontiguousarray(np.asarray(L0_attn_w_out, dtype=np.float32))
    bout = np.ascontiguousarray(np.asarray(L0_attn_b_out, dtype=np.float32)).reshape(1, D)
    wfc = np.ascontiguousarray(np.asarray(L0_ffn_W_fc, dtype=np.float32))
    lnw = np.ascontiguousarray(np.asarray(L0_ffn_ln_w, dtype=np.float32).reshape(32, 128).T)
    lnb = np.ascontiguousarray(np.asarray(L0_ffn_ln_b, dtype=np.float32).reshape(32, 128).T)
    wproj = np.ascontiguousarray(np.asarray(L0_ffn_W_proj, dtype=np.float32))

    theta = 1.0 / (10000.0 ** (np.arange(0, DH, 2, dtype=np.float64) / DH))
    ang = np.concatenate([np.outer(np.arange(T, dtype=np.float64), theta)] * 2, axis=1)
    cosT = np.cos(ang).T
    sinT = np.sin(ang).T
    cos128 = np.concatenate([cosT, cosT], axis=0)
    sin128 = np.concatenate([sinT, sinT], axis=0)
    cos_hi = cos128.astype(np.float32)
    cos_lo = (cos128 - cos_hi).astype(np.float32)
    sin_hi = sin128.astype(np.float32)
    sin_lo = (sin128 - sin_hi).astype(np.float32)

    if _NC_CACHE[0] is None:
        _NC_CACHE[0] = build_program()
    nc = _NC_CACHE[0]

    in_maps = []
    for c in range(8):
        b, hh = c // 2, c % 2
        sh = hh * OWN
        in_maps.append({
            "x": np.ascontiguousarray(np.roll(x[b], -sh, axis=0)),
            "wqkv": wqkv, "wout": wout, "bout": bout, "wfc": wfc,
            "lnw": lnw, "lnb": lnb, "wproj": wproj,
            "cos_hi": np.ascontiguousarray(np.roll(cos_hi, -sh, axis=1)),
            "cos_lo": np.ascontiguousarray(np.roll(cos_lo, -sh, axis=1)),
            "sin_hi": np.ascontiguousarray(np.roll(sin_hi, -sh, axis=1)),
            "sin_lo": np.ascontiguousarray(np.roll(sin_lo, -sh, axis=1)),
        })

    res = run_bass_kernel_spmd(nc, in_maps, core_ids=list(range(8)))
    out = np.empty((B, T, D), dtype=np.float32)
    for c in range(8):
        b, hh = c // 2, c % 2
        out[b, hh * OWN:(hh + 1) * OWN, :] = res.results[c]["out"]
    return out
